# version 23
# baseline (speedup 1.0000x reference)
import numpy as np
import ml_dtypes

import concourse.bass as bass
import concourse.mybir as mybir
from concourse.bass_utils import run_bass_kernel_spmd

F32 = mybir.dt.float32
BF16 = mybir.dt.bfloat16
AF = mybir.ActivationFunctionType
ALU = mybir.AluOpType
AX = mybir.AxisListType

B, T, C = 4, 4096, 1024
H = 16
NPAIR = 8
NT = 32
NOWN = 16
OWN = 2048
E = 64
IM = 512
ISH = 2048
NEX = 68
KC = 8
EPS = 1e-5
MOE_SCALE = 2.5

_DEBUG = False


class _Sem:
    __slots__ = ("h", "n")


class _P:
    def __init__(self, nc):
        self.nc = nc
        self.em = {"sync": [], "tensor": [], "vector": [], "scalar": [], "gpsimd": []}

    def sem(self, name):
        s = _Sem()
        s.h = self.nc.alloc_semaphore(name)
        s.n = 0
        return s

    def emit(self, eng, waits, incs, fn):
        waits = tuple((s, v) for (s, v) in waits if v > 0)
        incs = tuple(incs)

        def run(e):
            for s, v in waits:
                e.wait_ge(s.h, v)
            inst = fn(e)
            for s, dd in incs:
                inst = inst.then_inc(s.h, dd)

        self.em[eng].append(run)
        for s, dd in incs:
            s.n += dd


def _ap(t, off, dims):
    return bass.AP(t, off, [list(x) for x in dims])


def _pdim(tensor):
    return list(tensor[:, :].ap[0])


def _build():
    nc = bass.Bass("TRN2", target_bir_lowering=False, detect_race_conditions=False)

    d = {}
    d["xp"] = nc.dram_tensor("xp", [T, C], F32, kind="ExternalInput")
    d["wqkv"] = nc.dram_tensor("wqkv", [C, 3 * C], F32, kind="ExternalInput")
    d["bqkv_pp"] = nc.dram_tensor("bqkv_pp", [128, 24], F32, kind="ExternalInput")
    d["wout"] = nc.dram_tensor("wout", [C, C], F32, kind="ExternalInput")
    d["bout_b"] = nc.dram_tensor("bout_b", [128, C], F32, kind="ExternalInput")
    d["bvb"] = nc.dram_tensor("bvb", [128, C], F32, kind="ExternalInput")
    d["rwt"] = nc.dram_tensor("rwt", [C, E], F32, kind="ExternalInput")
    d["cbb"] = nc.dram_tensor("cbb", [128, E], F32, kind="ExternalInput")
    d["upw"] = nc.dram_tensor("upw", [E, C, IM], F32, kind="ExternalInput")
    d["dnw"] = nc.dram_tensor("dnw", [E, IM, C], F32, kind="ExternalInput")
    d["shup"] = nc.dram_tensor("shup", [C, ISH], F32, kind="ExternalInput")
    d["shdn"] = nc.dram_tensor("shdn", [ISH, C], F32, kind="ExternalInput")
    d["maskA"] = nc.dram_tensor("maskA", [128, 128], BF16, kind="ExternalInput")
    d["maskB"] = nc.dram_tensor("maskB", [128, 128], BF16, kind="ExternalInput")
    d["identb"] = nc.dram_tensor("identb", [128, 128], BF16, kind="ExternalInput")
    d["identf"] = nc.dram_tensor("identf", [128, 128], F32, kind="ExternalInput")
    d["ones64"] = nc.dram_tensor("ones64", [1, 64], F32, kind="ExternalInput")
    d["negeighth"] = nc.dram_tensor("negeighth", [1, 128], F32, kind="ExternalInput")
    d["out"] = nc.dram_tensor("out", [OWN, C], F32, kind="ExternalOutput")
    comb_dram = nc.dram_tensor("comb_dram", [E, OWN], BF16, kind="Internal")

    dbg = {}
    if _DEBUG:
        dbg["xn_dbg"] = nc.dram_tensor("xn_dbg", [128, 32768], BF16, kind="ExternalOutput")
        dbg["v_dbg"] = nc.dram_tensor("v_dbg", [128, 33280], BF16, kind="ExternalOutput")
        dbg["outf_dbg"] = nc.dram_tensor("outf_dbg", [128, 16384], BF16, kind="ExternalOutput")
        dbg["acc3_dbg"] = nc.dram_tensor("acc3_dbg", [128, 16384], F32, kind="ExternalOutput")
        dbg["comb_dbg"] = nc.dram_tensor("comb_dbg", [E, OWN], BF16, kind="ExternalOutput")
        dbg["k_dbg"] = nc.dram_tensor("k_dbg", [128, 4096], BF16, kind="ExternalOutput")

    AR = 209664
    arena = nc.alloc_sbuf_tensor("arena", [128, AR], mybir.dt.uint8)
    base = nc.lookup_mloc(arena).addr

    def at(name, shape, dt, off):
        return nc.alloc_sbuf_tensor_at(name, shape, dt, offset=base + off)

    xn_own = at("xn_own", [128, 16384], BF16, 0)
    xn_oth = at("xn_oth", [128, 16384], BF16, 32768)
    V_sb = at("V_sb", [128, 33280], BF16, 65536)
    wv_st = [at(f"wv_st{k}", [128, 1024], F32, 65536 + 4096 * k) for k in range(2)]
    acc = at("acc", [128, 16384], F32, 65536)
    junk = at("junk", [128, 1024], F32, 132096)
    outf = at("outf", [128, 16384], BF16, 132096)
    slot_f32, up_bf, dn_bf = [None] * 4, [None] * 4, [None] * 4
    for k, off in enumerate((132096, 148480, 32768, 49152)):
        slot_f32[k] = at(f"slotf{k}", [128, 4096], F32, off)
        up_bf[k] = at(f"upbf{k}", [128, 4096], BF16, off)
        dn_bf[k] = at(f"dnbf{k}", [128, 4096], BF16, off + 8192)
    wv_bf = at("wv_bf", [128, 8192], BF16, 148480)
    wq_bf = at("wq_bf", [128, 1024], BF16, 189440)
    wk_bf = at("wk_bf", [128, 1024], BF16, 193536)
    K_pair = [at(f"kp{k}", [128, 4096], BF16, 164864 + 8192 * k) for k in range(2)]
    wout_bf = at("wout_bf", [128, 8192], BF16, 164864)
    h_sb = [at(f"h{k}", [128, 2048], BF16, 164864 + 4096 * k) for k in range(4)]
    Q_pair = [at(f"qp{k}", [128, 2048], BF16, 181248 + 4096 * k) for k in range(2)]
    bvb_sb = at("bvb_sb", [128, 1024], F32, 181248)
    bout_sb = at("bout_sb", [128, 1024], F32, 181248)
    rt_sc = [at(f"rt_sc{k}", [128, 64], F32, 185344 + 256 * k) for k in range(2)]
    rt_sfc = at("rt_sfc", [128, 64], F32, 185856)
    rt_cur = at("rt_cur", [128, 64], F32, 186112)
    rt_s64 = at("rt_s64", [128, 64], F32, 186368)
    rt_g8 = at("rt_g8", [128, 64], F32, 186624)
    rt_fin = [at(f"rt_fin{j}", [128, 64], BF16, 186880 + 128 * j) for j in range(16)]
    rt_cbtile = [at(f"rt_cbt{k}", [64, 128], BF16, 188928 + 256 * k) for k in range(2)]
    cb_sb = [at(f"cb{k}", [128, 512], BF16, 181248 + 1024 * k) for k in range(4)]
    hs_t1 = [at(f"hs_t1{k}", [128, 512], BF16, 185344 + 1024 * k) for k in range(2)]
    ob_sb = at("ob_sb", [128, 1024], F32, 186368)
    ob_sb = at("ob_sb2", [128, 1024], F32, 181248)
    ws = [at(f"ws{k}", [128, 1024], F32, 189440 + 4096 * k) for k in range(2)]
    ws4 = [at(f"ws4_{k}", [128, 1024], F32, 189440 + 4096 * k) for k in range(2)] + \
          [at(f"ws4_{k+2}", [128, 1024], F32, 197632 + 4096 * k) for k in range(2)]
    x_stage = ws
    xn_tm = [at(f"xntm{k}", [128, 1024], BF16, 197632 + 2048 * k) for k in range(2)]
    Pt = [at(f"pt{k}", [128, 512], BF16, 197632 + 1024 * k) for k in range(3)]
    linv = at("linv", [1, 512], F32, 200704)
    lb_sb = [at(f"lb{k}", [64, 512], F32, 202752) for k in range(1)]
    rwt_bf = at("rwt_bf", [128, 512], BF16, 204800)
    maskA = at("maskA_s", [128, 128], BF16, 205824)
    maskB = at("maskB_s", [128, 128], BF16, 206080)
    identb = at("identb_s", [128, 128], BF16, 206336)
    identf = at("identf_s", [128, 128], F32, 206592)
    ones64 = at("ones64_s", [1, 64], F32, 207104)
    negeighth = at("ne_s", [1, 128], F32, 207360)
    bqkv_pp = at("bqkv_s", [128, 24], F32, 207872)
    cbb_sb = at("cbb_s", [128, 64], F32, 208000)
    so = 208256
    st_s1 = [at(f"s1_{k}", [128, 1], F32, so + 32 * k) for k in range(2)]
    st_s2 = [at(f"s2_{k}", [128, 1], F32, so + 64 + 32 * k) for k in range(2)]
    st_m = [at(f"m_{k}", [128, 1], F32, so + 128 + 32 * k) for k in range(2)]
    st_nm = [at(f"nm_{k}", [128, 1], F32, so + 192 + 32 * k) for k in range(2)]
    st_var = [at(f"var_{k}", [128, 1], F32, so + 256 + 32 * k) for k in range(2)]
    st_rv = [at(f"rv_{k}", [128, 1], F32, so + 320 + 32 * k) for k in range(2)]
    st_r = [at(f"r_{k}", [128, 1], F32, so + 384 + 32 * k) for k in range(2)]
    st_nmr = [at(f"nmr_{k}", [128, 1], F32, so + 448 + 32 * k) for k in range(2)]
    qm_sb = at("qm_sb", [128, 1], F32, so + 512)
    km_sb = at("km_sb", [128, 1], F32, so + 544)
    rt_th = at("rt_th", [128, 1], F32, so + 576)
    rt_ds = at("rt_ds", [128, 1], F32, so + 608)
    negmu = at("negmu", [128, 16], F32, 209280)
    mu_sb = at("mu_sb", [1, 16], F32, 209344)

    ps_o = nc.alloc_psum_tensor("ps_o", [128, 2048], F32)
    ps_s = [nc.alloc_psum_tensor(f"ps_s{k}", [128, 512], F32) for k in range(2)]
    ps_a = nc.alloc_psum_tensor("ps_a", [128, 512], F32)
    ps_b = nc.alloc_psum_tensor("ps_b", [128, 512], F32)
    ps_tb = [t.bitcast(BF16) for t in ps_s]

    p = _P(nc)
    EM = p.emit
    sX = p.sem("sX"); sSq = p.sem("sSq"); sV1 = p.sem("sV1"); sApp = p.sem("sApp")
    sTr = p.sem("sTr"); sEv = p.sem("sEv"); sW = p.sem("sW"); sWS = p.sem("sWS")
    sC = p.sem("sC"); sCg = p.sem("sCg"); sVp = p.sem("sVp"); sVe = p.sem("sVe")
    sKQ = p.sem("sKQ"); sKQe = p.sem("sKQe"); sS = p.sem("sS"); sExp = p.sem("sExp")
    sMsk = p.sem("sMsk"); sPV = p.sem("sPV"); sNv = p.sem("sNv"); sNs = p.sem("sNs")
    sNt = p.sem("sNt"); sY = p.sem("sY"); sAcc = p.sem("sAcc"); sRtT = p.sem("sRtT")
    sRtS = p.sem("sRtS"); sRtV = p.sem("sRtV"); sUp = p.sem("sUp"); sRel = p.sem("sRel")
    sH = p.sem("sH"); sDn = p.sem("sDn"); sDnE = p.sem("sDnE"); sMoeT = p.sem("sMoeT")
    sSt = p.sem("sSt"); sLv = p.sem("sLv"); sVs = p.sem("sVs"); sRtC = p.sem("sRtC")
    sX2 = [p.sem("sX0"), p.sem("sX1")]
    sWrt = p.sem("sWrt")
    sWoS = [p.sem("sWoS2"), p.sem("sWoS3")]
    sWv = [p.sem("sWv0"), p.sem("sWv1")]
    sWo = [p.sem("sWo0"), p.sem("sWo1")]
    sWkq = p.sem("sWkq")
    sWsl = [p.sem(f"sWsl{k}") for k in range(4)]
    sWs2k = p.sem("sWs2k")
    sWs2kB = p.sem("sWs2kB")
    sCmb = [p.sem("sCmb0"), p.sem("sCmb1")]
    sCb4 = [p.sem(f"sCb{k}") for k in range(4)]
    sXr2 = [p.sem("sXr0"), p.sem("sXr1")]
    sDbg = p.sem("sDbg")

    xp_t = d["xp"]
    wqkv_t = d["wqkv"]

    def _consts(e):
        for dst, key in ((maskA, "maskA"), (maskB, "maskB"), (identb, "identb"),
                         (identf, "identf"), (ones64, "ones64"),
                         (negeighth, "negeighth"), (bqkv_pp, "bqkv_pp")):
            e.dma_start(out=dst[:, :], in_=d[key].ap()).then_inc(sW.h, 16)
        return e.dma_start(out=cbb_sb[:, :], in_=d["cbb"].ap())
    EM("sync", [], [(sW, 16)], _consts)
    sW.n += 16 * 7
    consts_w = sW.n

    def _warm(e):
        e.activation(out=st_r[0][:, :], in_=identf[:, 0:1], func=AF.Sqrt)
        e.activation(out=st_r[0][:, :], in_=identf[:, 0:1], func=AF.Exp)
        e.activation(out=st_r[0][:, :], in_=identf[:, 0:1], func=AF.Sigmoid)
        return e.activation(out=st_r[0][:, :], in_=identf[:, 0:1], func=AF.Relu)
    EM("scalar", [(sW, consts_w)], [(sSq, 1)], _warm)
    warm_v = sSq.n

    EM("sync", [], [(sW, 16)],
       lambda e: e.dma_start(out=bvb_sb[:, :], in_=d["bvb"].ap()))
    bvb_w = sW.n
    wv_cast = [0] * KC
    for c_ in range(KC):
        sl = c_ % 2
        w = []
        if c_ >= 2:
            w.append((sCg, wv_cast[c_ - 2]))
        wsrc = _ap(wqkv_t, 128 * c_ * 3072 + 2048, [[3072, 128], [1, 1024]])
        EM("gpsimd", w, [(sWv[sl], 16)],
           lambda e, sl=sl, wsrc=wsrc: e.dma_start(out=wv_st[sl][:, :], in_=wsrc))
        wv_w = sWv[sl].n
        EM("gpsimd", [(sWv[sl], wv_w)], [(sCg, 1)],
           lambda e, sl=sl, c_=c_: e.tensor_copy(out=wv_bf[:, 1024 * c_:1024 * (c_ + 1)],
                                                 in_=wv_st[sl][:, :]))
        wv_cast[c_] = sCg.n
    wv_done = sCg.n
    EM("gpsimd", [], [(sCg, 1)],
       lambda e: e.memset(_ap(V_sb, 64, [_pdim(V_sb), [1040, NT], [65, H]]), 1.0))
    vones = sCg.n

    ve_chk = [0] * NT

    def emit_vp(j):
        sl = j % 2
        src = xn_own if j < NOWN else xn_oth
        jj = j if j < NOWN else j - NOWN
        w = [(sCg, wv_done), (sEv, ev_chk[j])]
        if j >= 2:
            w.append((sVe, ve_chk[j - 2]))

        def _vproj(e, sl=sl, src=src, jj=jj):
            r = None
            for c_ in range(KC):
                lhs = src[:, 2048 * c_ + 128 * jj:2048 * c_ + 128 * (jj + 1)]
                for hf in range(2):
                    r = e.matmul(ps_o[:, 1024 * sl + 512 * hf:1024 * sl + 512 * (hf + 1)],
                                 lhs, wv_bf[:, 1024 * c_ + 512 * hf:1024 * c_ + 512 * (hf + 1)],
                                 start=(c_ == 0), stop=(c_ == KC - 1))
            return r
        EM("tensor", w, [(sVp, 1)], _vproj)
        vp_v = sVp.n

        EM("vector", [(sVp, vp_v), (sW, bvb_w), (sCg, vones)], [(sVe, 1)],
           lambda e, sl=sl, j=j: e.scalar_tensor_tensor(
               out=_ap(V_sb, 1040 * j, [_pdim(V_sb), [65, H], [1, 64]]),
               in0=_ap(ps_o.ap().tensor, 1024 * sl, [_pdim(ps_o), [64, H], [1, 64]]),
               scalar=0.0,
               in1=_ap(bvb_sb.ap().tensor, 0, [_pdim(bvb_sb), [64, H], [1, 64]]),
               op0=ALU.add, op1=ALU.add))
        ve_chk[j] = sVe.n

    VP_LAG = 11
    app_chk = [0] * NT
    tr_chk = [0] * NT
    ev_chk = [0] * NT
    for j in range(NT):
        sl = j % 2
        w = [(sApp, app_chk[j - 2])] if j >= 2 else []
        EM("sync", w, [(sX2[sl], 16)],
           lambda e, j=j, sl=sl: e.dma_start(out=x_stage[sl][:, :],
                                             in_=xp_t.ap()[128 * j:128 * (j + 1), :]))
        xv = sX2[sl].n
        EM("vector", [(sX2[sl], xv)], [(sV1, 1)],
           lambda e, sl=sl: e.reduce_sum(out=st_s1[sl][:, :], in_=x_stage[sl][:, :], axis=AX.X))
        red_v = sV1.n
        EM("scalar", [(sX2[sl], xv)], [(sSq, 1)],
           lambda e, sl=sl: e.activation(out=junk[:, :], in_=x_stage[sl][:, :],
                                         func=AF.Square, accum_out=st_s2[sl][:, :]))
        sq_v = sSq.n

        vb = sVs.n

        def _stats(e, sl=sl, vb=vb):
            e.tensor_scalar(out=st_m[sl][:, :], in0=st_s1[sl][:, :], scalar1=1.0 / C,
                            scalar2=None, op0=ALU.mult).then_inc(sVs.h, 1)
            e.tensor_scalar(out=st_nm[sl][:, :], in0=st_s1[sl][:, :], scalar1=-1.0 / C,
                            scalar2=None, op0=ALU.mult)
            e.wait_ge(sVs.h, vb + 1)
            e.tensor_scalar(out=st_var[sl][:, :], in0=st_m[sl][:, :],
                            scalar1=st_m[sl][:, 0:1], scalar2=None, op0=ALU.mult).then_inc(sVs.h, 1)
            e.wait_ge(sVs.h, vb + 2)
            e.scalar_tensor_tensor(out=st_var[sl][:, :], in0=st_s2[sl][:, :], scalar=1.0 / C,
                                   in1=st_var[sl][:, :], op0=ALU.mult,
                                   op1=ALU.subtract).then_inc(sVs.h, 1)
            e.wait_ge(sVs.h, vb + 3)
            e.tensor_single_scalar(out=st_var[sl][:, :], in_=st_var[sl][:, :], scalar=EPS,
                                   op=ALU.add).then_inc(sVs.h, 1)
            e.wait_ge(sVs.h, vb + 4)
            return e.reciprocal(out=st_rv[sl][:, :], in_=st_var[sl][:, :])
        sVs.n += 4
        EM("vector", [(sSq, sq_v), (sV1, red_v)], [(sV1, 1)], _stats)
        v1_v = sV1.n

        EM("scalar", [(sV1, v1_v)], [(sSq, 1)],
           lambda e, sl=sl: e.activation(out=st_r[sl][:, :], in_=st_rv[sl][:, :],
                                         func=AF.Sqrt))
        sq2_v = sSq.n

        vb2 = sVs.n

        def _apply(e, sl=sl, vb2=vb2):
            e.tensor_scalar(out=st_nmr[sl][:, :], in0=st_nm[sl][:, :],
                            scalar1=st_r[sl][:, 0:1], scalar2=None,
                            op0=ALU.mult).then_inc(sVs.h, 1)
            e.wait_ge(sVs.h, vb2 + 1)
            return e.tensor_scalar(out=xn_tm[sl][:, :], in0=x_stage[sl][:, :],
                                   scalar1=st_r[sl][:, 0:1], scalar2=st_nmr[sl][:, 0:1],
                                   op0=ALU.mult, op1=ALU.add)
        sVs.n += 1
        EM("vector", [(sSq, sq2_v)], [(sApp, 1)], _apply)
        app_chk[j] = sApp.n

        w = [(sApp, app_chk[j]), (sW, consts_w)]
        if j >= 2:
            w.append((sEv, ev_chk[j - 2]))

        def _trs(e, sl=sl):
            r = None
            for c_ in range(KC):
                r = e.transpose(out=ps_tb[sl][:, 128 * c_:128 * (c_ + 1)],
                                in_=xn_tm[sl][:, 128 * c_:128 * (c_ + 1)],
                                identity=identb[:, :])
            return r
        EM("tensor", w, [(sTr, 1)], _trs)
        tr_chk[j] = sTr.n

        dst = xn_own if j < NOWN else xn_oth
        jj = j if j < NOWN else j - NOWN
        EM("vector", [(sTr, tr_chk[j])], [(sEv, 1)],
           lambda e, sl=sl, dst=dst, jj=jj: e.tensor_copy(
               out=_ap(dst, 128 * jj, [_pdim(dst), [2048, KC], [1, 128]]),
               in_=ps_tb[sl][:, 0:1024]))
        ev_chk[j] = sEv.n
        if j >= VP_LAG:
            emit_vp(j - VP_LAG)
    p0_ev = sEv.n

    EM("gpsimd", [(sTr, tr_chk[NT - 1])], [(sWrt, 16)],
       lambda e: e.dma_start(out=junk[:, 0:512],
                             in_=_ap(d["rwt"], 0, [[64, 128], [128 * 64, KC], [1, 64]])))
    rwt_w = sWrt.n
    EM("scalar", [(sWrt, rwt_w)], [(sC, 1)],
       lambda e: e.copy(out=rwt_bf[:, :], in_=junk[:, 0:512]))
    rwt_c = sC.n

    for j in range(NT - VP_LAG, NT):
        emit_vp(j)
    p1_ve = sVe.n

    if _DEBUG:
        def _dbgxn(e):
            e.dma_start(out=dbg["xn_dbg"].ap()[:, 0:16384], in_=xn_own[:, :]).then_inc(sDbg.h, 16)
            return e.dma_start(out=dbg["xn_dbg"].ap()[:, 16384:32768], in_=xn_oth[:, :])
        EM("sync", [(sEv, p0_ev)], [(sDbg, 16)], _dbgxn)
        sDbg.n += 16
        EM("sync", [(sVe, p1_ve)], [(sDbg, 16)],
           lambda e: e.dma_start(out=dbg["v_dbg"].ap(), in_=V_sb[:, :]))

    def pieces_of(kc):
        jq = kc % 16
        qs = 128 * jq
        out = []
        for bnk in range(qs // 512, 4):
            off = max(512 * bnk, qs)
            out.append((off, 512 * (bnk + 1) - off))
        return out

    kc_last = [min(15, 4 * m + 3) + 16 for m in range(4)]

    gpar = [0]
    pipe_cons = []
    pipe_mmv = []

    pipe_bufs = [ps_s[0], ps_s[1], ps_a, ps_b]
    pipe_b16 = [t.bitcast(BF16) for t in pipe_bufs]
    PD = 4

    def pipe_mm(extra_waits, mmfn, inc_sem):
        par = gpar[0]
        w = list(extra_waits)
        if par >= PD:
            w.append(pipe_cons[par - PD])
        EM("tensor", w, [(inc_sem, 1)], lambda e, par=par: mmfn(e, pipe_bufs[par % PD]))
        assert len(pipe_mmv) == par
        pipe_mmv.append((inc_sem, inc_sem.n))
        gpar[0] = par + 1
        return par

    def pipe_consume(par, extra_waits, cfn, eng="scalar", sem=None):
        sem = sem or sExp
        w = list(extra_waits) + [pipe_mmv[par]]
        EM(eng, w, [(sem, 1)], lambda e, par=par: cfn(e, pipe_bufs[par % PD]))
        assert len(pipe_cons) == par
        pipe_cons.append((sem, sem.n))

    kqe_chk = [0] * NPAIR
    nrm_chk = [0] * (H + 1)
    wkq_cast_chk = [0] * NPAIR
    pt_read = {0: 0, 1: 0, 2: 0}
    pt_cnt = [0]

    for pr in range(NPAIR):
        psl = pr % 2
        srcq = _ap(wqkv_t, 128 * pr, [[3072, 128], [128 * 3072, KC], [1, 128]])
        srck = _ap(wqkv_t, 1024 + 128 * pr, [[3072, 128], [128 * 3072, KC], [1, 128]])
        w = [(sEv, p0_ev)]
        if pr >= 1:
            w.append((sKQ, sKQ.n))
        def _wkq(e, srcq=srcq, srck=srck):
            e.dma_start(out=ws[0][:, :], in_=srcq).then_inc(sWkq.h, 16)
            return e.dma_start(out=ws[1][:, :], in_=srck)
        EM("sync", w, [(sWkq, 16)], _wkq)
        sWkq.n += 16
        wkq_w = sWkq.n
        wc = [(sWkq, wkq_w)]
        if pr >= 1:
            wc.append((sKQ, sKQ.n))
        EM("scalar", wc, [(sC, 1)], lambda e: e.copy(out=wq_bf[:, :], in_=ws[0][:, :]))
        EM("scalar", [(sWkq, wkq_w)], [(sC, 1)], lambda e: e.copy(out=wk_bf[:, :], in_=ws[1][:, :]))
        wkq_cast_chk[pr] = sC.n
        cast_v = sC.n

        for pc in range(12):
            is_q = pc >= 8

            def _proj(e, pb, pc=pc, is_q=is_q, psl=psl):
                r = None
                for c_ in range(KC):
                    if is_q:
                        lhs = wq_bf[:, 128 * c_:128 * (c_ + 1)]
                        rs = xn_own[:, 2048 * c_ + 512 * (pc - 8):2048 * c_ + 512 * (pc - 7)]
                    else:
                        lhs = wk_bf[:, 128 * c_:128 * (c_ + 1)]
                        src = xn_own if pc < 4 else xn_oth
                        o_ = (pc % 4) * 512
                        rs = src[:, 2048 * c_ + o_:2048 * c_ + o_ + 512]
                    r = e.matmul(pb[:, 0:512], lhs, rs, start=(c_ == 0), stop=(c_ == KC - 1))
                return r
            par = pipe_mm([(sC, cast_v)], _proj, sKQ)

            if is_q:
                def _pev(e, pb, pc=pc, psl=psl, pr=pr):
                    return e.tensor_scalar(
                        out=Q_pair[psl][:, 512 * (pc - 8):512 * (pc - 7)], in0=pb[:, 0:512],
                        scalar1=bqkv_pp[:, pr:pr + 1], scalar2=None, op0=ALU.add)
            else:
                def _pev(e, pb, pc=pc, psl=psl, pr=pr):
                    return e.tensor_scalar(
                        out=K_pair[psl][:, 512 * pc:512 * (pc + 1)], in0=pb[:, 0:512],
                        scalar1=bqkv_pp[:, 8 + pr:9 + pr], scalar2=None, op0=ALU.add)
            pipe_consume(par, [], _pev, eng="vector", sem=sKQe)
        kqe_chk[pr] = sKQe.n

        if _DEBUG and pr == 0:
            EM("sync", [(sKQe, kqe_chk[0])], [(sDbg, 16)],
               lambda e: e.dma_start(out=dbg["k_dbg"].ap(), in_=K_pair[0][:, :]))

        EM("vector", [(sKQe, kqe_chk[pr])], [(sNv, 1)],
           lambda e, psl=psl: e.reduce_max(out=qm_sb[:, :], in_=Q_pair[psl][:, :],
                                           axis=AX.X, apply_absolute_value=True))
        EM("vector", [], [(sNv, 1)],
           lambda e, psl=psl: e.reduce_max(out=km_sb[:, :], in_=K_pair[psl][:, :],
                                           axis=AX.X, apply_absolute_value=True))
        nv_v = sNv.n
        for hh in range(2):
            hg = 2 * pr + hh
            par = pipe_mm([(sNv, nv_v), (sW, consts_w)],
                          lambda e, pb, hh=hh: e.matmul(
                              pb[0:1, 0:1], qm_sb[64 * hh:64 * (hh + 1), 0:1],
                              km_sb[64 * hh:64 * (hh + 1), 0:1], start=True, stop=True),
                          sNt)
            pipe_consume(par, [],
                         lambda e, pb, hg=hg: e.tensor_copy(out=mu_sb[0:1, hg:hg + 1],
                                                            in_=pb[0:1, 0:1]),
                         eng="vector", sem=sNv)
            mu_c = sNv.n
            par = pipe_mm([(sNv, mu_c)],
                          lambda e, pb, hg=hg: e.matmul(pb[:, 1:2], negeighth[0:1, :],
                                                        mu_sb[0:1, hg:hg + 1],
                                                        start=True, stop=True),
                          sNt)
            pipe_consume(par, [],
                         lambda e, pb, hg=hg: e.copy(out=negmu[:, hg:hg + 1],
                                                     in_=pb[:, 1:2]),
                         eng="scalar", sem=sNs)
        mu_s = sNs.n

        for hh in range(2):
            hg = 2 * pr + hh
            plist = []
            for kc in range(NT):
                for pi, (qoff, n) in enumerate(pieces_of(kc)):
                    plist.append((kc, pi, qoff, n))

            def emit_pv(item):
                ptsl2, kc, qoff, n, exp_v, msk_v = item
                wpv = [(sExp, exp_v), (sVe, ve_chk[kc])]
                if msk_v:
                    wpv.append((sMsk, msk_v))
                if kc == 0:
                    wpv.append((sNv, nrm_chk[hg]))

                def _pv(e, ptsl2=ptsl2, kc=kc, qoff=qoff, n=n, hg=hg):
                    stop = (kc == kc_last[qoff // 512])
                    return e.matmul(ps_o[0:65, qoff:qoff + n],
                                    V_sb[:, 1040 * kc + 65 * hg:1040 * kc + 65 * hg + 65],
                                    Pt[ptsl2][:, 0:n], start=(kc == 0), stop=stop)
                EM("tensor", wpv, [(sPV, 1)], _pv)
                pt_read[ptsl2] = sPV.n

            pend2 = []
            for (kc, pi, qoff, n) in plist:
                srcm = maskA if kc < 16 else maskB
                ptsl = gpar[0] % 3

                def _smm(e, pb, hh=hh, kc=kc, qoff=qoff, n=n, psl=psl):
                    return e.matmul(pb[:, 0:n],
                                    K_pair[psl][64 * hh:64 * (hh + 1),
                                                128 * kc:128 * (kc + 1)],
                                    Q_pair[psl][64 * hh:64 * (hh + 1), qoff:qoff + n],
                                    start=True, stop=True)
                par = pipe_mm([(sKQe, kqe_chk[pr]), (sNs, mu_s)], _smm, sS)
                wexp = [(sPV, pt_read[ptsl])]

                def _exp(e, pb, ptsl=ptsl, n=n, hg=hg):
                    return e.activation(out=Pt[ptsl][:, 0:n], in_=pb[:, 0:n], func=AF.Exp,
                                        scale=0.125, bias=negmu[:, hg:hg + 1])
                pipe_consume(par, wexp, _exp)
                exp_v = sExp.n
                msk_v = 0
                if pi == 0:
                    EM("vector", [(sExp, exp_v)], [(sMsk, 1)],
                       lambda e, ptsl=ptsl, srcm=srcm: e.tensor_tensor(
                           out=Pt[ptsl][:, 0:128], in0=Pt[ptsl][:, 0:128],
                           in1=srcm[:, :], op=ALU.mult))
                    msk_v = sMsk.n
                pend2.append((ptsl, kc, qoff, n, exp_v, msk_v))
                if len(pend2) > 2:
                    emit_pv(pend2.pop(0))
            for item in pend2:
                emit_pv(item)
            head_pv = sPV.n
            for m_ in range(4):
                EM("vector", [(sPV, head_pv)], [(sLv, 1)],
                   lambda e, m_=m_: e.reciprocal(out=linv[0:1, :],
                                                 in_=ps_o[64:65, 512 * m_:512 * (m_ + 1)]))
                lv_v = sLv.n

                def _lbmm(e, pb):
                    return e.matmul(pb[0:64, 0:512], ones64[0:1, :], linv[0:1, :],
                                    start=True, stop=True)
                par = pipe_mm([(sLv, lv_v)], _lbmm, sNt)
                lbsl = 0
                wlb = []
                if m_ >= 1:
                    wlb.append((sNv, sNv.n))

                def _lbev(e, pb, lbsl=lbsl):
                    return e.copy(out=lb_sb[lbsl][:, :], in_=pb[0:64, 0:512])
                pipe_consume(par, wlb, _lbev)
                ev_v = sExp.n
                wev = [(sExp, ev_v)] + ([(sC, rwt_c)] if pr == 0 else [])
                EM("vector", wev, [(sNv, 1)],
                   lambda e, m_=m_, hh=hh, pr=pr, lbsl=lbsl: e.tensor_tensor(
                       out=outf[64 * hh:64 * (hh + 1),
                                2048 * pr + 512 * m_:2048 * pr + 512 * (m_ + 1)],
                       in0=ps_o[0:64, 512 * m_:512 * (m_ + 1)],
                       in1=lb_sb[lbsl][:, :], op=ALU.mult))
            nrm_chk[hg + 1] = sNv.n
    attn_nrm = sNv.n
    attn_exp = sExp.n

    if _DEBUG:
        EM("sync", [(sNv, attn_nrm)], [(sDbg, 16)],
           lambda e: e.dma_start(out=dbg["outf_dbg"].ap(), in_=outf[:, :]))

    EM("sync", [(sNv, attn_nrm)], [(sW, 16)],
       lambda e: e.dma_start(out=bout_sb[:, :], in_=d["bout_b"].ap()))
    bout_w = sW.n
    for si, k_ in enumerate((2, 3)):
        EM("sync", [(sKQ, sKQ.n)], [(sWoS[si], 16)],
           lambda e, k_=k_, si=si: e.dma_start(
               out=slot_f32[k_][:, :],
               in_=_ap(d["wout"], 512 * si * 1024, [[1024, 128], [128 * 1024, 4], [1, 1024]])))
    wo_wsl = {2: sWoS[0].n, 3: sWoS[1].n}
    wout_cast = [0] * KC
    for c_ in range(KC):
        k_ = 2 if c_ < 4 else 3
        EM("scalar", [(sWoS[k_ - 2], wo_wsl[k_]), (sExp, attn_exp)], [(sC, 1)],
           lambda e, c_=c_, k_=k_: e.copy(
               out=wout_bf[:, 1024 * c_:1024 * (c_ + 1)],
               in_=slot_f32[k_][:, 1024 * (c_ % 4):1024 * (c_ % 4 + 1)]))
        wout_cast[c_] = sC.n

    acc_chk = [0] * NOWN

    def emit_yproj(j):
        sl = j % 2
        w = [(sC, wout_cast[KC - 1])]
        if j == 0:
            w.append((sNv, attn_nrm))
        if j >= 2:
            w.append((sAcc, acc_chk[j - 2]))

        def _yproj(e, sl=sl, j=j):
            r = None
            for pr2 in range(NPAIR):
                lhs = outf[:, 2048 * pr2 + 128 * j:2048 * pr2 + 128 * (j + 1)]
                for hf in range(2):
                    r = e.matmul(ps_o[:, 1024 * sl + 512 * hf:1024 * sl + 512 * (hf + 1)],
                                 lhs,
                                 wout_bf[:, 1024 * pr2 + 512 * hf:1024 * pr2 + 512 * (hf + 1)],
                                 start=(pr2 == 0), stop=(pr2 == NPAIR - 1))
            return r
        EM("tensor", w, [(sY, 1)], _yproj)
        y_v = sY.n
        EM("vector", [(sY, y_v), (sW, bout_w)], [(sAcc, 1)],
           lambda e, sl=sl, j=j: e.scalar_tensor_tensor(
               out=acc[:, 1024 * j:1024 * (j + 1)],
               in0=ps_o[:, 1024 * sl:1024 * (sl + 1)], scalar=0.0,
               in1=bout_sb[:, :], op0=ALU.add, op1=ALU.add))
        acc_chk[j] = sAcc.n

    rt_lg = [0] * NOWN
    rt_sg = [0] * NOWN
    rt_rv = [0] * NOWN
    rt_vc = [0]
    attn_drain = [(sExp, attn_exp), (sNv, attn_nrm), (sNs, sNs.n), (sKQe, sKQe.n)]

    def _route_j(j):
        def _route(e):
            g8 = rt_g8
            scj = rt_sc[j % 2]

            def step(inst):
                inst.then_inc(sRtC.h, 1)
                rt_vc[0] += 1
                e.wait_ge(sRtC.h, rt_vc[0])
                return inst
            step(e.tensor_tensor(out=rt_sfc[:, :], in0=scj[:, :], in1=cbb_sb[:, :], op=ALU.add))
            sfc3 = rt_sfc[:, :].rearrange("p (g x) -> p g x", x=8)
            step(e.reduce_max(out=g8[:, 0:8], in_=sfc3, axis=AX.X))
            step(e.tensor_tensor(out=rt_s64[:, :].rearrange("p (g x) -> p g x", x=8),
                                 in0=sfc3, in1=g8[:, 0:8].unsqueeze(2).broadcast_to((128, 8, 8)),
                                 op=ALU.is_ge))
            step(e.reduce_sum(out=g8[:, 8:16],
                              in_=rt_s64[:, :].rearrange("p (g x) -> p g x", x=8), axis=AX.X))
            step(e.scalar_tensor_tensor(out=rt_cur[:, :], in0=rt_s64[:, :], scalar=-1e30,
                                        in1=rt_sfc[:, :], op0=ALU.mult, op1=ALU.add))
            step(e.reduce_max(out=g8[:, 16:24],
                              in_=rt_cur[:, :].rearrange("p (g x) -> p g x", x=8), axis=AX.X))
            step(e.tensor_single_scalar(out=g8[:, 24:32], in_=g8[:, 8:16], scalar=2.0, op=ALU.is_ge))
            step(e.tensor_tensor(out=g8[:, 32:40], in0=g8[:, 0:8], in1=g8[:, 16:24], op=ALU.subtract))
            step(e.tensor_tensor(out=g8[:, 32:40], in0=g8[:, 32:40], in1=g8[:, 24:32], op=ALU.mult))
            step(e.tensor_tensor(out=g8[:, 32:40], in0=g8[:, 32:40], in1=g8[:, 16:24], op=ALU.add))
            step(e.tensor_tensor(out=g8[:, 40:48], in0=g8[:, 0:8], in1=g8[:, 32:40], op=ALU.add))
            for jj in range(8):
                e.tensor_scalar(out=rt_s64[:, 8 * jj:8 * (jj + 1)], in0=g8[:, 40:48],
                                scalar1=g8[:, 40 + jj:41 + jj], scalar2=None, op0=ALU.is_lt)
            step(e.tensor_single_scalar(out=rt_s64[:, 0:8], in_=rt_s64[:, 0:8],
                                        scalar=0.0, op=ALU.add))
            step(e.reduce_sum(out=g8[:, 48:56],
                              in_=_ap(rt_s64.ap().tensor, 0, [_pdim(rt_s64), [1, 8], [8, 8]]),
                              axis=AX.X))
            step(e.tensor_single_scalar(out=g8[:, 56:64], in_=g8[:, 48:56], scalar=3.0, op=ALU.is_le))
            gm3 = g8[:, 56:64].unsqueeze(2).broadcast_to((128, 8, 8))
            step(e.tensor_tensor(out=rt_cur[:, :].rearrange("p (g x) -> p g x", x=8),
                                 in0=sfc3, in1=gm3, op=ALU.mult))
            for it in range(8):
                step(e.reduce_max(out=rt_th[:, :], in_=rt_cur[:, :], axis=AX.X))
                if it < 7:
                    step(e.tensor_scalar(out=rt_s64[:, :], in0=rt_cur[:, :],
                                         scalar1=rt_th[:, 0:1], scalar2=None, op0=ALU.is_ge))
                    step(e.scalar_tensor_tensor(out=rt_cur[:, :], in0=rt_s64[:, :], scalar=-1e30,
                                                in1=rt_cur[:, :], op0=ALU.mult, op1=ALU.add))
            step(e.tensor_tensor(out=rt_s64[:, :].rearrange("p (g x) -> p g x", x=8),
                                 in0=sfc3, in1=gm3, op=ALU.mult))
            step(e.tensor_scalar(out=rt_s64[:, :], in0=rt_s64[:, :], scalar1=rt_th[:, 0:1],
                                 scalar2=None, op0=ALU.is_ge))
            step(e.tensor_tensor(out=rt_cur[:, :], in0=scj[:, :], in1=rt_s64[:, :], op=ALU.mult))
            step(e.reduce_sum(out=rt_ds[:, :], in_=rt_cur[:, :], axis=AX.X))
            step(e.tensor_single_scalar(out=rt_ds[:, :], in_=rt_ds[:, :], scalar=1e-20, op=ALU.add))
            step(e.reciprocal(out=rt_ds[:, :], in_=rt_ds[:, :]))
            return e.tensor_scalar(out=rt_fin[j][:, :], in0=rt_cur[:, :],
                                   scalar1=rt_ds[:, 0:1], scalar2=MOE_SCALE,
                                   op0=ALU.mult, op1=ALU.mult)
        return _route

    for j in range(NOWN):
        co = 64 * (j % 2)
        wlg = [(sC, rwt_c)]
        if j < 2:
            wlg += attn_drain
        else:
            wlg.append((sRtS, rt_sg[j - 2]))

        def _lg(e, j=j, co=co):
            r = None
            for c_ in range(KC):
                r = e.matmul(ps_s[0][:, co:co + 64],
                             xn_own[:, 2048 * c_ + 128 * j:2048 * c_ + 128 * (j + 1)],
                             rwt_bf[:, 64 * c_:64 * (c_ + 1)],
                             start=(c_ == 0), stop=(c_ == KC - 1))
            return r
        EM("tensor", wlg, [(sRtT, 1)], _lg)
        rt_lg[j] = sRtT.n
        wsg = [(sRtT, rt_lg[j])]
        if j >= 2:
            wsg.append((sRtV, rt_rv[j - 2]))
        EM("scalar", wsg, [(sRtS, 1)],
           lambda e, j=j, co=co: e.activation(out=rt_sc[j % 2][:, :],
                                              in_=ps_s[0][:, co:co + 64], func=AF.Sigmoid))
        rt_sg[j] = sRtS.n
        EM("vector", [(sRtS, rt_sg[j]), (sW, consts_w)], [(sRtV, 1)], _route_j(j))
        rt_rv[j] = sRtV.n
        if j >= 2:
            emit_yproj(j - 2)
    emit_yproj(NOWN - 2)
    emit_yproj(NOWN - 1)
    acc_init = sAcc.n
    yproj_done = sY.n

    if _DEBUG:
        EM("sync", [(sAcc, acc_init)], [(sDbg, 16)],
           lambda e: e.dma_start(out=dbg["acc3_dbg"].ap(), in_=acc[:, :]))

    rt_ct_chk = [0] * NOWN
    rt_ev_chk = [0] * NOWN
    rt_cb_chk = [0] * NOWN
    cmb_tile = [None] * NOWN
    for j in range(NOWN):
        wt_ = [(sRtV, rt_rv[j])]
        if j >= 1:
            wt_.append((sRtS, rt_cb_chk[j - 1]))
        EM("tensor", wt_, [(sRtT, 1)],
           lambda e, j=j: e.transpose(out=ps_tb[1][0:64, 0:128], in_=rt_fin[j][:, :],
                                      identity=identb[:, :]))
        rt_ct_chk[j] = sRtT.n
        csl = j % 2
        wce = [(sRtT, rt_ct_chk[j])]
        if j >= 2:
            wce.append((sCmb[csl], rt_ev_chk[j - 2]))
        EM("scalar", wce, [(sRtS, 1)],
           lambda e, csl=csl: e.copy(out=rt_cbtile[csl][:, :], in_=ps_tb[1][0:64, 0:128]))
        rt_cb_chk[j] = sRtS.n
        EM("scalar", [(sRtS, rt_cb_chk[j])], [(sCmb[csl], 16)],
           lambda e, csl=csl, j=j: e.dma_start(
               out=_ap(comb_dram, 128 * j, [[OWN, 64], [1, 128]]),
               in_=rt_cbtile[csl][:, :]))
        rt_ev_chk[j] = sCmb[csl].n
        cmb_tile[j] = [(sCmb[0], sCmb[0].n), (sCmb[1], sCmb[1].n)]
    comb_done = [(sCmb[0], sCmb[0].n), (sCmb[1], sCmb[1].n)]
    rt_final_evac = sRtS.n
    rt_lg_done = sRtT.n

    if _DEBUG:
        EM("sync", comb_done, [(sDbg, 16)],
           lambda e: e.dma_start(out=dbg["comb_dbg"].ap(), in_=comb_dram.ap()))

    pex = [("r", i) for i in range(E)] + [("s", i) for i in range(4)]
    up_cast = [0] * NEX
    dn_cast = [0] * NEX
    dn_ws = [0]
    dn_ws_cast = []

    def emit_loads_pair(ea, eb):
        descs = []
        for ei in (ea, eb):
            if ei is None or ei >= NEX:
                continue
            kind, idx = pex[ei]
            k = (ei + 2) % 4
            if kind == "r":
                srcu4 = [_ap(d["upw"], idx * C * IM + 2 * q_ * 128 * IM,
                             [[IM, 128], [128 * IM, 2], [1, IM]]) for q_ in range(4)]
                dsc = [_ap(d["dnw"], idx * IM * C + p_ * 128 * C, [[C, 128], [1, C]])
                       for p_ in range(4)]
            else:
                srcu4 = [_ap(d["shup"], idx * IM + 2 * q_ * 128 * ISH,
                             [[ISH, 128], [128 * ISH, 2], [1, IM]]) for q_ in range(4)]
                dsc = [_ap(d["shdn"], (idx * IM + p_ * 128) * C, [[C, 128], [1, C]])
                       for p_ in range(4)]
            w = []
            if ei >= 4:
                w.append((sDn, 16 * ((ei - 4) // 2 + 1)))
            else:
                w.append((sY, yproj_done) if k < 2
                         else (sC, wout_cast[3] if k == 2 else wout_cast[KC - 1]))
            for q_ in range(4):
                EM("gpsimd", w if q_ == 0 else [], [(sWsl[k], 16)],
                   lambda e, s=srcu4[q_], k=k, q_=q_: e.dma_start(
                       out=slot_f32[k][:, 1024 * q_:1024 * (q_ + 1)], in_=s))
            descs.append((ei, k, dsc, sWsl[k].n))
        for ei, k, dsc, upw_v in descs:
            EM("scalar", [(sWsl[k], upw_v)], [(sCg, 1)],
               lambda e, k=k: e.copy(out=up_bf[k][:, :], in_=slot_f32[k][:, :]))
            up_cast[ei] = sCg.n
        for ei, k, dsc, upw_v in descs:
            for p_ in range(4):
                wsl = dn_ws[0] % 4
                wd = []
                if dn_ws[0] >= 4:
                    wd.append((sCg, dn_ws_cast[dn_ws[0] - 4]))
                elif dn_ws[0] in (0, 1):
                    wd.append((sC, wout_cast[KC - 1]))
                elif dn_ws[0] in (2, 3):
                    wd.append((sRtT, rt_lg_done))
                odd_q = False
                dn_eng = "scalar" if odd_q else "sync"
                dn_sem = sWs2kB if odd_q else sWs2k
                EM(dn_eng, wd, [(dn_sem, 16)],
                   lambda e, s=dsc[p_], wsl=wsl: e.dma_start(out=ws4[wsl][:, :], in_=s))
                d_v = dn_sem.n
                wc = [(dn_sem, d_v)]
                if p_ == 0:
                    wc.append((sCg, up_cast[ei]))
                EM("scalar", wc, [(sCg, 1)],
                   lambda e, k=k, p_=p_, wsl=wsl: e.copy(
                       out=dn_bf[k][:, 1024 * p_:1024 * (p_ + 1)], in_=ws4[wsl][:, :]))
                dn_ws_cast.append(sCg.n)
                dn_ws[0] += 1
            dn_cast[ei] = sCg.n

    emit_loads_pair(0, 1)
    emit_loads_pair(2, 3)

    cb_sched = []
    for g in range(NEX // 2):
        for tq in range(4):
            for eo in range(2):
                ei = 2 * g + eo
                if pex[ei][0] == "r":
                    cb_sched.append((ei, tq))
    cb_consume_after = [0] * len(cb_sched)
    cb_w = {}
    cb_issued = [0]

    def issue_cb(upto):
        while cb_issued[0] < min(upto, len(cb_sched)):
            kk = cb_issued[0]
            ei, tq = cb_sched[kk]
            idx = pex[ei][1]
            slot = kk % 4
            w = list(cmb_tile[4 * tq + 3])
            if kk >= 4:
                w.append((sH, cb_consume_after[kk - 4]))
            EM("scalar", w, [(sCb4[slot], 16)],
               lambda e, idx=idx, tq=tq, slot=slot: e.dma_start(
                   out=cb_sb[slot][:, :],
                   in_=_ap(comb_dram, idx * OWN + 512 * tq, [[0, 128], [1, 512]])))
            cb_w[(ei, tq)] = (sCb4[slot], sCb4[slot].n)
            cb_issued[0] += 1

    cb_pos = [0]
    hs_cnt = [0]
    st_up = [0]
    st_rel = [0]
    st_dn = [0]

    dn_after_q = {}
    h_ready_q = {}

    def emit_up_quarter(qi):
        g, tq = qi // 4, qi % 4
        e0, e1 = 2 * g, 2 * g + 1
        nonlocal_state = None
        for eo, ei in enumerate((e0, e1)):
            kind, idx = pex[ei]
            k = (ei + 2) % 4
            hbuf = h_sb[2 * (tq % 2) + eo]
            if kind == "r":
                issue_cb(cb_pos[0] + 4)
                cslot = cb_pos[0] % 4
            for ic in range(4):
                pb = ps_a if st_up[0] % 2 == 0 else ps_b
                w = [(sCg, up_cast[ei])]
                if st_up[0] >= 2:
                    w.append((sRel, st_rel[0] - 1))

                def _up(e, pb=pb, k=k, ic=ic, tq=tq):
                    r = None
                    for c_ in range(KC):
                        r = e.matmul(
                            pb[:, 0:512],
                            up_bf[k][:, 512 * c_ + 128 * ic:512 * c_ + 128 * (ic + 1)],
                            xn_own[:, 2048 * c_ + 512 * tq:2048 * c_ + 512 * (tq + 1)],
                            start=(c_ == 0), stop=(c_ == KC - 1))
                    return r
                EM("tensor", w, [(sUp, 1)], _up)
                st_up[0] += 1
                up_v = sUp.n
                wr = [(sUp, up_v)]
                if qi >= 2:
                    wr.append((sDn, dn_after_q[qi - 2]))
                EM("scalar", wr, [(sRel, 1)],
                   lambda e, pb=pb, hbuf=hbuf, ic=ic: e.activation(
                       out=hbuf[:, 512 * ic:512 * (ic + 1)], in_=pb[:, 0:512],
                       func=AF.Relu))
                st_rel[0] += 1
                rel_v = sRel.n
                if kind == "r":
                    tsl = hs_cnt[0] % 2
                    wv_ = [(sRel, rel_v), cb_w[(ei, tq)]]
                    if hs_cnt[0] >= 2:
                        wv_.append((sH, sH.n - 1))
                    EM("vector", wv_, [(sVs, 1)],
                       lambda e, hbuf=hbuf, ic=ic, cslot=cslot, tsl=tsl:
                       e.tensor_tensor(out=hs_t1[tsl][:, :],
                                       in0=hbuf[:, 512 * ic:512 * (ic + 1)],
                                       in1=cb_sb[cslot][:, :], op=ALU.mult))
                    EM("gpsimd", [(sVs, sVs.n)], [(sH, 1)],
                       lambda e, hbuf=hbuf, ic=ic, tsl=tsl:
                       e.tensor_tensor(out=hbuf[:, 512 * ic:512 * (ic + 1)],
                                       in0=hs_t1[tsl][:, :],
                                       in1=hbuf[:, 512 * ic:512 * (ic + 1)],
                                       op=ALU.mult))
                    hs_cnt[0] += 1
                else:
                    EM("vector", [(sRel, rel_v)], [(sH, 1)],
                       lambda e, hbuf=hbuf, ic=ic:
                       e.tensor_tensor(out=hbuf[:, 512 * ic:512 * (ic + 1)],
                                       in0=hbuf[:, 512 * ic:512 * (ic + 1)],
                                       in1=hbuf[:, 512 * ic:512 * (ic + 1)],
                                       op=ALU.mult))
            if kind == "r":
                cb_consume_after[cb_pos[0]] = sH.n
                cb_pos[0] += 1
        h_ready_q[qi] = sH.n

    def emit_down_quarter(qi):
        g, tq = qi // 4, qi % 4
        e0, e1 = 2 * g, 2 * g + 1
        for tt in range(4):
            pd_off = 1024 * (tt % 2)
            w = [(sH, h_ready_q[qi]), (sCg, max(dn_cast[e0], dn_cast[e1]))]
            if st_dn[0] >= 2:
                w.append((sDnE, sDnE.n - 1))
            if qi == 0:
                w.append((sAcc, acc_init))

            def _dn(e, e0=e0, e1=e1, tq=tq, tt=tt, pd_off=pd_off):
                r = None
                for eo2, ei2 in enumerate((e0, e1)):
                    k2 = (ei2 + 2) % 4
                    hb = h_sb[2 * (tq % 2) + eo2]
                    last_e = ei2 == e1
                    for ic in range(4):
                        lhs = hb[:, 512 * ic + 128 * tt:512 * ic + 128 * (tt + 1)]
                        for hf in range(2):
                            r = e.matmul(
                                ps_o[:, pd_off + 512 * hf:pd_off + 512 * (hf + 1)],
                                lhs,
                                dn_bf[k2][:, 1024 * ic + 512 * hf:1024 * ic + 512 * (hf + 1)],
                                start=(eo2 == 0 and ic == 0),
                                stop=(last_e and ic == 3))
                return r
            EM("tensor", w, [(sDn, 1)], _dn)
            st_dn[0] += 1
            dn_v = sDn.n
            tile = 4 * tq + tt
            EM("vector", [(sDn, dn_v)], [(sDnE, 1)],
               lambda e, pd_off=pd_off, tile=tile: e.scalar_tensor_tensor(
                   out=acc[:, 1024 * tile:1024 * (tile + 1)],
                   in0=ps_o[:, pd_off:pd_off + 1024], scalar=0.0,
                   in1=acc[:, 1024 * tile:1024 * (tile + 1)],
                   op0=ALU.add, op1=ALU.add))
        dn_after_q[qi] = sDn.n
        if tq == 3:
            emit_loads_pair(2 * g + 4, 2 * g + 5)

    NQ = 4 * (NEX // 2)
    emit_up_quarter(0)
    for qi in range(1, NQ):
        emit_up_quarter(qi)
        emit_down_quarter(qi - 1)
    emit_down_quarter(NQ - 1)
    moe_all = sDnE.n

    adds = [0] * NOWN
    last_st = 0
    for j in range(NOWN):
        sl = j % 2
        w = [(sDnE, (NEX // 2 - 1) * 16 + j + 1), (sCg, sCg.n)]
        if j >= 2:
            w.append((sAcc, adds[j - 2]))
        EM("sync", w, [(sXr2[sl], 16)],
           lambda e, sl=sl, j=j: e.dma_start(out=x_stage[sl][:, :],
                                             in_=xp_t.ap()[128 * j:128 * (j + 1), :]))
        xr_v = sXr2[sl].n
        wv_ = [(sXr2[sl], xr_v)]
        if j >= 1:
            wv_.append((sSt, last_st))
        EM("vector", wv_, [(sAcc, 1)],
           lambda e, sl=sl, j=j: e.tensor_tensor(out=ob_sb[:, :],
                                                 in0=acc[:, 1024 * j:1024 * (j + 1)],
                                                 in1=x_stage[sl][:, :], op=ALU.add))
        adds[j] = sAcc.n
        EM("sync", [(sAcc, adds[j])], [(sSt, 16)],
           lambda e, j=j: e.dma_start(out=d["out"].ap()[128 * j:128 * (j + 1), :],
                                      in_=ob_sb[:, :]))
        last_st = sSt.n

    EM("sync", [(sSt, sSt.n), (sDbg, sDbg.n)], [], lambda e: e.nop())

    with nc.Block() as block:
        @block.sync
        def _(e):
            for fn in p.em["sync"]:
                fn(e)

        @block.tensor
        def _(e):
            for fn in p.em["tensor"]:
                fn(e)

        @block.vector
        def _(e):
            for fn in p.em["vector"]:
                fn(e)

        @block.scalar
        def _(e):
            for fn in p.em["scalar"]:
                fn(e)

        @block.gpsimd
        def _(e):
            for fn in p.em["gpsimd"]:
                fn(e)

    return nc


def _perm_rows(s):
    own = [2 * i + s for i in range(16)]
    oth = [2 * i + (1 - s) for i in range(16)]
    rows = []
    for t_ in own + oth:
        rows.extend(range(128 * t_, 128 * (t_ + 1)))
    return np.array(rows, dtype=np.int64)


def kernel(L0_attn_x, L0_attn_w_qkv, L0_attn_b_qkv, L0_attn_w_out, L0_attn_b_out,
           L0_ffn_router_w, L0_ffn_correction_bias, L0_ffn_up_w, L0_ffn_down_w,
           L0_ffn_shared_up_w, L0_ffn_shared_down_w):
    x = np.ascontiguousarray(np.asarray(L0_attn_x, dtype=np.float32))
    wqkv = np.ascontiguousarray(np.asarray(L0_attn_w_qkv, dtype=np.float32))
    bqkv = np.asarray(L0_attn_b_qkv, dtype=np.float32)
    wout = np.ascontiguousarray(np.asarray(L0_attn_w_out, dtype=np.float32))
    bout = np.asarray(L0_attn_b_out, dtype=np.float32)
    rw = np.asarray(L0_ffn_router_w, dtype=np.float32)
    cbias = np.asarray(L0_ffn_correction_bias, dtype=np.float32)
    upw = np.ascontiguousarray(np.asarray(L0_ffn_up_w, dtype=np.float32))
    dnw = np.ascontiguousarray(np.asarray(L0_ffn_down_w, dtype=np.float32))
    shup = np.ascontiguousarray(np.asarray(L0_ffn_shared_up_w, dtype=np.float32))
    shdn = np.ascontiguousarray(np.asarray(L0_ffn_shared_down_w, dtype=np.float32))

    bqkv_pp = np.ascontiguousarray(bqkv.reshape(24, 128).T)
    bout_b = np.ascontiguousarray(np.broadcast_to(bout[None, :], (128, C)))
    bvb = np.ascontiguousarray(np.broadcast_to(bqkv[None, 2048:3072], (128, C)))
    rwt = np.ascontiguousarray(rw.T)
    cbb = np.ascontiguousarray(np.broadcast_to(cbias[None, :], (128, E)))
    tri = np.ascontiguousarray(
        np.triu(np.ones((128, 128), np.float32)).astype(ml_dtypes.bfloat16))
    zeros128 = np.zeros((128, 128), ml_dtypes.bfloat16)
    ones128 = np.ones((128, 128), ml_dtypes.bfloat16)
    identb = np.eye(128, dtype=ml_dtypes.bfloat16)
    identf = np.eye(128, dtype=np.float32)
    ones64 = np.ones((1, 64), np.float32)
    negeighth = np.full((1, 128), -0.125, np.float32)

    nc = _build()

    in_maps = []
    for c in range(8):
        b, s = c // 2, c % 2
        pr = _perm_rows(s)
        in_maps.append({
            "xp": np.ascontiguousarray(x[b][pr]),
            "wqkv": wqkv, "bqkv_pp": bqkv_pp, "wout": wout, "bout_b": bout_b,
            "bvb": bvb, "rwt": rwt, "cbb": cbb, "upw": upw, "dnw": dnw,
            "shup": shup, "shdn": shdn,
            "maskA": tri, "maskB": (zeros128 if s == 0 else ones128),
            "identb": identb, "identf": identf, "ones64": ones64,
            "negeighth": negeighth,
        })

    res = run_bass_kernel_spmd(nc, in_maps, core_ids=list(range(8)))

    out = np.empty((B, T, C), np.float32)
    for c in range(8):
        b, s = c // 2, c % 2
        o = res.results[c]["out"]
        for i in range(16):
            out[b, 128 * (2 * i + s):128 * (2 * i + s + 1)] = o[128 * i:128 * (i + 1)]
    return out


# version 25 (final)
# speedup vs baseline: 1.0385x; 1.0385x over previous
import numpy as np
import ml_dtypes

import concourse.bass as bass
import concourse.mybir as mybir
from concourse.bass_utils import run_bass_kernel_spmd

F32 = mybir.dt.float32
BF16 = mybir.dt.bfloat16
AF = mybir.ActivationFunctionType
ALU = mybir.AluOpType
AX = mybir.AxisListType

B, T, C = 4, 4096, 1024
H = 16
NPAIR = 8
NT = 32
NOWN = 16
OWN = 2048
E = 64
IM = 512
ISH = 2048
NEX = 68
KC = 8
EPS = 1e-5
MOE_SCALE = 2.5

_DEBUG = False


class _Sem:
    __slots__ = ("h", "n")


class _P:
    def __init__(self, nc):
        self.nc = nc
        self.em = {"sync": [], "tensor": [], "vector": [], "scalar": [], "gpsimd": []}

    def sem(self, name):
        s = _Sem()
        s.h = self.nc.alloc_semaphore(name)
        s.n = 0
        return s

    def emit(self, eng, waits, incs, fn):
        waits = tuple((s, v) for (s, v) in waits if v > 0)
        incs = tuple(incs)

        def run(e):
            for s, v in waits:
                e.wait_ge(s.h, v)
            inst = fn(e)
            for s, dd in incs:
                inst = inst.then_inc(s.h, dd)

        self.em[eng].append(run)
        for s, dd in incs:
            s.n += dd


def _ap(t, off, dims):
    return bass.AP(t, off, [list(x) for x in dims])


def _pdim(tensor):
    return list(tensor[:, :].ap[0])


def _build():
    nc = bass.Bass("TRN2", target_bir_lowering=False, detect_race_conditions=False)

    d = {}
    d["xp"] = nc.dram_tensor("xp", [T, C], F32, kind="ExternalInput")
    d["wqkv"] = nc.dram_tensor("wqkv", [C, 3 * C], F32, kind="ExternalInput")
    d["bqkv_pp"] = nc.dram_tensor("bqkv_pp", [128, 24], F32, kind="ExternalInput")
    d["wout"] = nc.dram_tensor("wout", [C, C], F32, kind="ExternalInput")
    d["bout_b"] = nc.dram_tensor("bout_b", [128, C], F32, kind="ExternalInput")
    d["bvb"] = nc.dram_tensor("bvb", [128, C], F32, kind="ExternalInput")
    d["rwt"] = nc.dram_tensor("rwt", [C, E], F32, kind="ExternalInput")
    d["cbb"] = nc.dram_tensor("cbb", [128, E], F32, kind="ExternalInput")
    d["upw"] = nc.dram_tensor("upw", [E, C, IM], F32, kind="ExternalInput")
    d["dnw"] = nc.dram_tensor("dnw", [E, IM, C], F32, kind="ExternalInput")
    d["shup"] = nc.dram_tensor("shup", [C, ISH], F32, kind="ExternalInput")
    d["shdn"] = nc.dram_tensor("shdn", [ISH, C], F32, kind="ExternalInput")
    d["maskA"] = nc.dram_tensor("maskA", [128, 128], BF16, kind="ExternalInput")
    d["maskB"] = nc.dram_tensor("maskB", [128, 128], BF16, kind="ExternalInput")
    d["identb"] = nc.dram_tensor("identb", [128, 128], BF16, kind="ExternalInput")
    d["identf"] = nc.dram_tensor("identf", [128, 128], F32, kind="ExternalInput")
    d["ones64"] = nc.dram_tensor("ones64", [1, 64], F32, kind="ExternalInput")
    d["negeighth"] = nc.dram_tensor("negeighth", [1, 128], F32, kind="ExternalInput")
    d["out"] = nc.dram_tensor("out", [OWN, C], F32, kind="ExternalOutput")
    comb_dram = nc.dram_tensor("comb_dram", [E, OWN], BF16, kind="Internal")

    dbg = {}
    if _DEBUG:
        dbg["xn_dbg"] = nc.dram_tensor("xn_dbg", [128, 32768], BF16, kind="ExternalOutput")
        dbg["v_dbg"] = nc.dram_tensor("v_dbg", [128, 33280], BF16, kind="ExternalOutput")
        dbg["outf_dbg"] = nc.dram_tensor("outf_dbg", [128, 16384], BF16, kind="ExternalOutput")
        dbg["acc3_dbg"] = nc.dram_tensor("acc3_dbg", [128, 16384], F32, kind="ExternalOutput")
        dbg["comb_dbg"] = nc.dram_tensor("comb_dbg", [E, OWN], BF16, kind="ExternalOutput")
        dbg["k_dbg"] = nc.dram_tensor("k_dbg", [128, 4096], BF16, kind="ExternalOutput")

    AR = 209664
    arena = nc.alloc_sbuf_tensor("arena", [128, AR], mybir.dt.uint8)
    base = nc.lookup_mloc(arena).addr

    def at(name, shape, dt, off):
        return nc.alloc_sbuf_tensor_at(name, shape, dt, offset=base + off)

    xn_own = at("xn_own", [128, 16384], BF16, 0)
    xn_oth = at("xn_oth", [128, 16384], BF16, 32768)
    V_sb = at("V_sb", [128, 33280], BF16, 65536)
    wv_st = [at(f"wv_st{k}", [128, 1024], F32, 65536 + 4096 * k) for k in range(2)]
    acc = at("acc", [128, 16384], F32, 65536)
    junk = at("junk", [128, 1024], F32, 132096)
    outf = at("outf", [128, 16384], BF16, 132096)
    slot_f32, up_bf, dn_bf = [None] * 4, [None] * 4, [None] * 4
    for k, off in enumerate((132096, 148480, 32768, 49152)):
        slot_f32[k] = at(f"slotf{k}", [128, 4096], F32, off)
        up_bf[k] = at(f"upbf{k}", [128, 4096], BF16, off)
        dn_bf[k] = at(f"dnbf{k}", [128, 4096], BF16, off + 8192)
    wv_bf = at("wv_bf", [128, 8192], BF16, 148480)
    wq_bf = at("wq_bf", [128, 1024], BF16, 189440)
    wk_bf = at("wk_bf", [128, 1024], BF16, 193536)
    K_pair = [at(f"kp{k}", [128, 4096], BF16, 164864 + 8192 * k) for k in range(2)]
    wout_bf = at("wout_bf", [128, 8192], BF16, 164864)
    h_sb = [at(f"h{k}", [128, 2048], BF16, 164864 + 4096 * k) for k in range(4)]
    Q_pair = [at(f"qp{k}", [128, 2048], BF16, 181248 + 4096 * k) for k in range(2)]
    bvb_sb = at("bvb_sb", [128, 1024], F32, 181248)
    bout_sb = at("bout_sb", [128, 1024], F32, 181248)
    rt_sc = [at(f"rt_sc{k}", [128, 64], F32, 185344 + 256 * k) for k in range(2)]
    rt_sfc = at("rt_sfc", [128, 64], F32, 185856)
    rt_cur = at("rt_cur", [128, 64], F32, 186112)
    rt_s64 = at("rt_s64", [128, 64], F32, 186368)
    rt_g8 = at("rt_g8", [128, 64], F32, 186624)
    rt_fin = [at(f"rt_fin{j}", [128, 64], BF16, 186880 + 128 * j) for j in range(16)]
    rt_cbtile = [at(f"rt_cbt{k}", [64, 128], BF16, 188928 + 256 * k) for k in range(2)]
    cb_sb = [at(f"cb{k}", [128, 512], BF16, 181248 + 1024 * k) for k in range(4)]
    hs_t1 = [at(f"hs_t1{k}", [128, 512], BF16, 185344 + 1024 * k) for k in range(2)]
    ob_sb = at("ob_sb", [128, 1024], F32, 186368)
    ob_sb = at("ob_sb2", [128, 1024], F32, 181248)
    ws = [at(f"ws{k}", [128, 1024], F32, 189440 + 4096 * k) for k in range(2)]
    ws4 = [at(f"ws4_{k}", [128, 1024], F32, 189440 + 4096 * k) for k in range(2)] + \
          [at(f"ws4_{k+2}", [128, 1024], F32, 197632 + 4096 * k) for k in range(2)]
    x_stage = ws
    xn_tm = [at(f"xntm{k}", [128, 1024], BF16, 197632 + 2048 * k) for k in range(2)]
    Pt = [at(f"pt{k}", [128, 512], BF16, 197632 + 1024 * k) for k in range(3)]
    linv = at("linv", [1, 512], F32, 200704)
    lb_sb = [at(f"lb{k}", [64, 512], F32, 202752) for k in range(1)]
    rwt_bf = at("rwt_bf", [128, 512], BF16, 204800)
    maskA = at("maskA_s", [128, 128], BF16, 205824)
    maskB = at("maskB_s", [128, 128], BF16, 206080)
    identb = at("identb_s", [128, 128], BF16, 206336)
    identf = at("identf_s", [128, 128], F32, 206592)
    ones64 = at("ones64_s", [1, 64], F32, 207104)
    negeighth = at("ne_s", [1, 128], F32, 207360)
    bqkv_pp = at("bqkv_s", [128, 24], F32, 207872)
    cbb_sb = at("cbb_s", [128, 64], F32, 208000)
    so = 208256
    st_s1 = [at(f"s1_{k}", [128, 1], F32, so + 32 * k) for k in range(2)]
    st_s2 = [at(f"s2_{k}", [128, 1], F32, so + 64 + 32 * k) for k in range(2)]
    st_m = [at(f"m_{k}", [128, 1], F32, so + 128 + 32 * k) for k in range(2)]
    st_nm = [at(f"nm_{k}", [128, 1], F32, so + 192 + 32 * k) for k in range(2)]
    st_var = [at(f"var_{k}", [128, 1], F32, so + 256 + 32 * k) for k in range(2)]
    st_rv = [at(f"rv_{k}", [128, 1], F32, so + 320 + 32 * k) for k in range(2)]
    st_r = [at(f"r_{k}", [128, 1], F32, so + 384 + 32 * k) for k in range(2)]
    st_nmr = [at(f"nmr_{k}", [128, 1], F32, so + 448 + 32 * k) for k in range(2)]
    qm_sb = at("qm_sb", [128, 1], F32, so + 512)
    km_sb = at("km_sb", [128, 1], F32, so + 544)
    rt_th = at("rt_th", [128, 1], F32, so + 576)
    rt_ds = at("rt_ds", [128, 1], F32, so + 608)
    negmu = at("negmu", [128, 16], F32, 209280)
    mu_sb = at("mu_sb", [1, 16], F32, 209344)

    ps_o = nc.alloc_psum_tensor("ps_o", [128, 2048], F32)
    ps_s = [nc.alloc_psum_tensor(f"ps_s{k}", [128, 512], F32) for k in range(2)]
    ps_a = nc.alloc_psum_tensor("ps_a", [128, 512], F32)
    ps_b = nc.alloc_psum_tensor("ps_b", [128, 512], F32)
    ps_tb = [t.bitcast(BF16) for t in ps_s]

    p = _P(nc)
    EM = p.emit
    sX = p.sem("sX"); sSq = p.sem("sSq"); sV1 = p.sem("sV1"); sApp = p.sem("sApp")
    sTr = p.sem("sTr"); sEv = p.sem("sEv"); sW = p.sem("sW"); sWS = p.sem("sWS")
    sC = p.sem("sC"); sCg = p.sem("sCg"); sVp = p.sem("sVp"); sVe = p.sem("sVe")
    sKQ = p.sem("sKQ"); sKQe = p.sem("sKQe"); sS = p.sem("sS"); sExp = p.sem("sExp")
    sMsk = p.sem("sMsk"); sPV = p.sem("sPV"); sNv = p.sem("sNv"); sNs = p.sem("sNs")
    sNt = p.sem("sNt"); sY = p.sem("sY"); sAcc = p.sem("sAcc"); sRtT = p.sem("sRtT")
    sRtS = p.sem("sRtS"); sRtV = p.sem("sRtV"); sUp = p.sem("sUp"); sRel = p.sem("sRel")
    sH = p.sem("sH"); sDn = p.sem("sDn"); sDnE = p.sem("sDnE"); sMoeT = p.sem("sMoeT")
    sSt = p.sem("sSt"); sLv = p.sem("sLv"); sVs = p.sem("sVs"); sRtC = p.sem("sRtC")
    sX2 = [p.sem("sX0"), p.sem("sX1")]
    sWrt = p.sem("sWrt")
    sWoS = [p.sem("sWoS2"), p.sem("sWoS3")]
    sWv = [p.sem("sWv0"), p.sem("sWv1")]
    sWo = [p.sem("sWo0"), p.sem("sWo1")]
    sWkq = p.sem("sWkq")
    sWsl = [p.sem(f"sWsl{k}") for k in range(4)]
    sWs2k = p.sem("sWs2k")
    sWs2kB = p.sem("sWs2kB")
    sCmb = [p.sem("sCmb0"), p.sem("sCmb1")]
    sCb4 = [p.sem(f"sCb{k}") for k in range(4)]
    sXr2 = [p.sem("sXr0"), p.sem("sXr1")]
    sDbg = p.sem("sDbg")

    xp_t = d["xp"]
    wqkv_t = d["wqkv"]

    def _consts(e):
        for dst, key in ((maskA, "maskA"), (maskB, "maskB"), (identb, "identb"),
                         (identf, "identf"), (ones64, "ones64"),
                         (negeighth, "negeighth"), (bqkv_pp, "bqkv_pp")):
            e.dma_start(out=dst[:, :], in_=d[key].ap()).then_inc(sW.h, 16)
        return e.dma_start(out=cbb_sb[:, :], in_=d["cbb"].ap())
    EM("sync", [], [(sW, 16)], _consts)
    sW.n += 16 * 7
    consts_w = sW.n

    def _warm(e):
        e.activation(out=st_r[0][:, :], in_=identf[:, 0:1], func=AF.Sqrt)
        e.activation(out=st_r[0][:, :], in_=identf[:, 0:1], func=AF.Exp)
        e.activation(out=st_r[0][:, :], in_=identf[:, 0:1], func=AF.Sigmoid)
        return e.activation(out=st_r[0][:, :], in_=identf[:, 0:1], func=AF.Relu)
    EM("scalar", [(sW, consts_w)], [(sSq, 1)], _warm)
    warm_v = sSq.n

    EM("sync", [], [(sW, 16)],
       lambda e: e.dma_start(out=bvb_sb[:, :], in_=d["bvb"].ap()))
    bvb_w = sW.n
    wv_cast = [0] * KC
    for c_ in range(KC):
        sl = c_ % 2
        w = []
        if c_ >= 2:
            w.append((sCg, wv_cast[c_ - 2]))
        wsrc = _ap(wqkv_t, 128 * c_ * 3072 + 2048, [[3072, 128], [1, 1024]])
        EM("gpsimd", w, [(sWv[sl], 16)],
           lambda e, sl=sl, wsrc=wsrc: e.dma_start(out=wv_st[sl][:, :], in_=wsrc))
        wv_w = sWv[sl].n
        EM("gpsimd", [(sWv[sl], wv_w)], [(sCg, 1)],
           lambda e, sl=sl, c_=c_: e.tensor_copy(out=wv_bf[:, 1024 * c_:1024 * (c_ + 1)],
                                                 in_=wv_st[sl][:, :]))
        wv_cast[c_] = sCg.n
    wv_done = sCg.n
    EM("gpsimd", [], [(sCg, 1)],
       lambda e: e.memset(_ap(V_sb, 64, [_pdim(V_sb), [1040, NT], [65, H]]), 1.0))
    vones = sCg.n

    ve_chk = [0] * NT

    def emit_vp(j):
        sl = j % 2
        src = xn_own if j < NOWN else xn_oth
        jj = j if j < NOWN else j - NOWN
        w = [(sCg, wv_done), (sEv, ev_chk[j])]
        if j >= 2:
            w.append((sVe, ve_chk[j - 2]))

        def _vproj(e, sl=sl, src=src, jj=jj):
            r = None
            for c_ in range(KC):
                lhs = src[:, 2048 * c_ + 128 * jj:2048 * c_ + 128 * (jj + 1)]
                for hf in range(2):
                    r = e.matmul(ps_o[:, 1024 * sl + 512 * hf:1024 * sl + 512 * (hf + 1)],
                                 lhs, wv_bf[:, 1024 * c_ + 512 * hf:1024 * c_ + 512 * (hf + 1)],
                                 start=(c_ == 0), stop=(c_ == KC - 1))
            return r
        EM("tensor", w, [(sVp, 1)], _vproj)
        vp_v = sVp.n

        EM("vector", [(sVp, vp_v), (sW, bvb_w), (sCg, vones)], [(sVe, 1)],
           lambda e, sl=sl, j=j: e.scalar_tensor_tensor(
               out=_ap(V_sb, 1040 * j, [_pdim(V_sb), [65, H], [1, 64]]),
               in0=_ap(ps_o.ap().tensor, 1024 * sl, [_pdim(ps_o), [64, H], [1, 64]]),
               scalar=0.0,
               in1=_ap(bvb_sb.ap().tensor, 0, [_pdim(bvb_sb), [64, H], [1, 64]]),
               op0=ALU.add, op1=ALU.add))
        ve_chk[j] = sVe.n

    VP_LAG = 11
    app_chk = [0] * NT
    tr_chk = [0] * NT
    ev_chk = [0] * NT
    for j in range(NT):
        sl = j % 2
        w = [(sApp, app_chk[j - 2])] if j >= 2 else []
        EM("sync", w, [(sX2[sl], 16)],
           lambda e, j=j, sl=sl: e.dma_start(out=x_stage[sl][:, :],
                                             in_=xp_t.ap()[128 * j:128 * (j + 1), :]))
        xv = sX2[sl].n
        EM("vector", [(sX2[sl], xv)], [(sV1, 1)],
           lambda e, sl=sl: e.reduce_sum(out=st_s1[sl][:, :], in_=x_stage[sl][:, :], axis=AX.X))
        red_v = sV1.n
        EM("scalar", [(sX2[sl], xv)], [(sSq, 1)],
           lambda e, sl=sl: e.activation(out=junk[:, :], in_=x_stage[sl][:, :],
                                         func=AF.Square, accum_out=st_s2[sl][:, :]))
        sq_v = sSq.n

        vb = sVs.n

        def _stats(e, sl=sl, vb=vb):
            e.tensor_scalar(out=st_m[sl][:, :], in0=st_s1[sl][:, :], scalar1=1.0 / C,
                            scalar2=None, op0=ALU.mult).then_inc(sVs.h, 1)
            e.tensor_scalar(out=st_nm[sl][:, :], in0=st_s1[sl][:, :], scalar1=-1.0 / C,
                            scalar2=None, op0=ALU.mult)
            e.wait_ge(sVs.h, vb + 1)
            e.tensor_scalar(out=st_var[sl][:, :], in0=st_m[sl][:, :],
                            scalar1=st_m[sl][:, 0:1], scalar2=None, op0=ALU.mult).then_inc(sVs.h, 1)
            e.wait_ge(sVs.h, vb + 2)
            e.scalar_tensor_tensor(out=st_var[sl][:, :], in0=st_s2[sl][:, :], scalar=1.0 / C,
                                   in1=st_var[sl][:, :], op0=ALU.mult,
                                   op1=ALU.subtract).then_inc(sVs.h, 1)
            e.wait_ge(sVs.h, vb + 3)
            e.tensor_single_scalar(out=st_var[sl][:, :], in_=st_var[sl][:, :], scalar=EPS,
                                   op=ALU.add).then_inc(sVs.h, 1)
            e.wait_ge(sVs.h, vb + 4)
            return e.reciprocal(out=st_rv[sl][:, :], in_=st_var[sl][:, :])
        sVs.n += 4
        EM("vector", [(sSq, sq_v), (sV1, red_v)], [(sV1, 1)], _stats)
        v1_v = sV1.n

        EM("scalar", [(sV1, v1_v)], [(sSq, 1)],
           lambda e, sl=sl: e.activation(out=st_r[sl][:, :], in_=st_rv[sl][:, :],
                                         func=AF.Sqrt))
        sq2_v = sSq.n

        vb2 = sVs.n

        def _apply(e, sl=sl, vb2=vb2):
            e.tensor_scalar(out=st_nmr[sl][:, :], in0=st_nm[sl][:, :],
                            scalar1=st_r[sl][:, 0:1], scalar2=None,
                            op0=ALU.mult).then_inc(sVs.h, 1)
            e.wait_ge(sVs.h, vb2 + 1)
            return e.tensor_scalar(out=xn_tm[sl][:, :], in0=x_stage[sl][:, :],
                                   scalar1=st_r[sl][:, 0:1], scalar2=st_nmr[sl][:, 0:1],
                                   op0=ALU.mult, op1=ALU.add)
        sVs.n += 1
        EM("vector", [(sSq, sq2_v)], [(sApp, 1)], _apply)
        app_chk[j] = sApp.n

        w = [(sApp, app_chk[j]), (sW, consts_w)]
        if j >= 2:
            w.append((sEv, ev_chk[j - 2]))

        def _trs(e, sl=sl):
            r = None
            for c_ in range(KC):
                r = e.transpose(out=ps_tb[sl][:, 128 * c_:128 * (c_ + 1)],
                                in_=xn_tm[sl][:, 128 * c_:128 * (c_ + 1)],
                                identity=identb[:, :])
            return r
        EM("tensor", w, [(sTr, 1)], _trs)
        tr_chk[j] = sTr.n

        dst = xn_own if j < NOWN else xn_oth
        jj = j if j < NOWN else j - NOWN
        EM("vector", [(sTr, tr_chk[j])], [(sEv, 1)],
           lambda e, sl=sl, dst=dst, jj=jj: e.tensor_copy(
               out=_ap(dst, 128 * jj, [_pdim(dst), [2048, KC], [1, 128]]),
               in_=ps_tb[sl][:, 0:1024]))
        ev_chk[j] = sEv.n
        if j >= VP_LAG:
            emit_vp(j - VP_LAG)
    p0_ev = sEv.n

    EM("gpsimd", [(sTr, tr_chk[NT - 1])], [(sWrt, 16)],
       lambda e: e.dma_start(out=junk[:, 0:512],
                             in_=_ap(d["rwt"], 0, [[64, 128], [128 * 64, KC], [1, 64]])))
    rwt_w = sWrt.n
    EM("scalar", [(sWrt, rwt_w)], [(sC, 1)],
       lambda e: e.copy(out=rwt_bf[:, :], in_=junk[:, 0:512]))
    rwt_c = sC.n

    for j in range(NT - VP_LAG, NT):
        emit_vp(j)
    p1_ve = sVe.n

    if _DEBUG:
        def _dbgxn(e):
            e.dma_start(out=dbg["xn_dbg"].ap()[:, 0:16384], in_=xn_own[:, :]).then_inc(sDbg.h, 16)
            return e.dma_start(out=dbg["xn_dbg"].ap()[:, 16384:32768], in_=xn_oth[:, :])
        EM("sync", [(sEv, p0_ev)], [(sDbg, 16)], _dbgxn)
        sDbg.n += 16
        EM("sync", [(sVe, p1_ve)], [(sDbg, 16)],
           lambda e: e.dma_start(out=dbg["v_dbg"].ap(), in_=V_sb[:, :]))

    def pieces_of(kc):
        jq = kc % 16
        qs = 128 * jq
        out = []
        for bnk in range(qs // 512, 4):
            off = max(512 * bnk, qs)
            out.append((off, 512 * (bnk + 1) - off))
        return out

    kc_last = [min(15, 4 * m + 3) + 16 for m in range(4)]

    gpar = [0]
    pipe_cons = []
    pipe_mmv = []

    pipe_bufs = [ps_s[0], ps_s[1], ps_a, ps_b]
    pipe_b16 = [t.bitcast(BF16) for t in pipe_bufs]
    PD = 4

    def pipe_mm(extra_waits, mmfn, inc_sem):
        par = gpar[0]
        w = list(extra_waits)
        if par >= PD:
            w.append(pipe_cons[par - PD])
        EM("tensor", w, [(inc_sem, 1)], lambda e, par=par: mmfn(e, pipe_bufs[par % PD]))
        assert len(pipe_mmv) == par
        pipe_mmv.append((inc_sem, inc_sem.n))
        gpar[0] = par + 1
        return par

    def pipe_consume(par, extra_waits, cfn, eng="scalar", sem=None):
        sem = sem or sExp
        w = list(extra_waits) + [pipe_mmv[par]]
        EM(eng, w, [(sem, 1)], lambda e, par=par: cfn(e, pipe_bufs[par % PD]))
        assert len(pipe_cons) == par
        pipe_cons.append((sem, sem.n))

    kqe_chk = [0] * NPAIR
    nrm_chk = [0] * (H + 1)
    wkq_cast_chk = [0] * NPAIR
    pt_read = {0: 0, 1: 0, 2: 0}
    pt_cnt = [0]

    for pr in range(NPAIR):
        psl = pr % 2
        srcq = _ap(wqkv_t, 128 * pr, [[3072, 128], [128 * 3072, KC], [1, 128]])
        srck = _ap(wqkv_t, 1024 + 128 * pr, [[3072, 128], [128 * 3072, KC], [1, 128]])
        w = [(sEv, p0_ev)]
        if pr >= 1:
            w.append((sKQ, sKQ.n))
        def _wkq(e, srcq=srcq, srck=srck):
            e.dma_start(out=ws[0][:, :], in_=srcq).then_inc(sWkq.h, 16)
            return e.dma_start(out=ws[1][:, :], in_=srck)
        EM("sync", w, [(sWkq, 16)], _wkq)
        sWkq.n += 16
        wkq_w = sWkq.n
        wc = [(sWkq, wkq_w)]
        if pr >= 1:
            wc.append((sKQ, sKQ.n))
        EM("scalar", wc, [(sC, 1)], lambda e: e.copy(out=wq_bf[:, :], in_=ws[0][:, :]))
        EM("scalar", [(sWkq, wkq_w)], [(sC, 1)], lambda e: e.copy(out=wk_bf[:, :], in_=ws[1][:, :]))
        wkq_cast_chk[pr] = sC.n
        cast_v = sC.n

        for pc in range(12):
            is_q = pc >= 8

            def _proj(e, pb, pc=pc, is_q=is_q, psl=psl):
                r = None
                for c_ in range(KC):
                    if is_q:
                        lhs = wq_bf[:, 128 * c_:128 * (c_ + 1)]
                        rs = xn_own[:, 2048 * c_ + 512 * (pc - 8):2048 * c_ + 512 * (pc - 7)]
                    else:
                        lhs = wk_bf[:, 128 * c_:128 * (c_ + 1)]
                        src = xn_own if pc < 4 else xn_oth
                        o_ = (pc % 4) * 512
                        rs = src[:, 2048 * c_ + o_:2048 * c_ + o_ + 512]
                    r = e.matmul(pb[:, 0:512], lhs, rs, start=(c_ == 0), stop=(c_ == KC - 1))
                return r
            par = pipe_mm([(sC, cast_v)], _proj, sKQ)

            if is_q:
                def _pev(e, pb, pc=pc, psl=psl, pr=pr):
                    return e.tensor_scalar(
                        out=Q_pair[psl][:, 512 * (pc - 8):512 * (pc - 7)], in0=pb[:, 0:512],
                        scalar1=bqkv_pp[:, pr:pr + 1], scalar2=None, op0=ALU.add)
            else:
                def _pev(e, pb, pc=pc, psl=psl, pr=pr):
                    return e.tensor_scalar(
                        out=K_pair[psl][:, 512 * pc:512 * (pc + 1)], in0=pb[:, 0:512],
                        scalar1=bqkv_pp[:, 8 + pr:9 + pr], scalar2=None, op0=ALU.add)
            pipe_consume(par, [], _pev, eng="vector", sem=sKQe)
        kqe_chk[pr] = sKQe.n

        if _DEBUG and pr == 0:
            EM("sync", [(sKQe, kqe_chk[0])], [(sDbg, 16)],
               lambda e: e.dma_start(out=dbg["k_dbg"].ap(), in_=K_pair[0][:, :]))

        EM("vector", [(sKQe, kqe_chk[pr])], [(sNv, 1)],
           lambda e, psl=psl: e.reduce_max(out=qm_sb[:, :], in_=Q_pair[psl][:, :],
                                           axis=AX.X, apply_absolute_value=True))
        EM("vector", [], [(sNv, 1)],
           lambda e, psl=psl: e.reduce_max(out=km_sb[:, :], in_=K_pair[psl][:, :],
                                           axis=AX.X, apply_absolute_value=True))
        nv_v = sNv.n
        for hh in range(2):
            hg = 2 * pr + hh
            par = pipe_mm([(sNv, nv_v), (sW, consts_w)],
                          lambda e, pb, hh=hh: e.matmul(
                              pb[0:1, 0:1], qm_sb[64 * hh:64 * (hh + 1), 0:1],
                              km_sb[64 * hh:64 * (hh + 1), 0:1], start=True, stop=True),
                          sNt)
            pipe_consume(par, [],
                         lambda e, pb, hg=hg: e.tensor_copy(out=mu_sb[0:1, hg:hg + 1],
                                                            in_=pb[0:1, 0:1]),
                         eng="vector", sem=sNv)
            mu_c = sNv.n
            par = pipe_mm([(sNv, mu_c)],
                          lambda e, pb, hg=hg: e.matmul(pb[:, 1:2], negeighth[0:1, :],
                                                        mu_sb[0:1, hg:hg + 1],
                                                        start=True, stop=True),
                          sNt)
            pipe_consume(par, [],
                         lambda e, pb, hg=hg: e.copy(out=negmu[:, hg:hg + 1],
                                                     in_=pb[:, 1:2]),
                         eng="scalar", sem=sNs)
        mu_s = sNs.n

        for hh in range(2):
            hg = 2 * pr + hh
            plist = []
            for kc in range(NT):
                for pi, (qoff, n) in enumerate(pieces_of(kc)):
                    plist.append((kc, pi, qoff, n))

            def emit_pv(item):
                ptsl2, kc, qoff, n, exp_v, msk_v = item
                wpv = [(sExp, exp_v), (sVe, ve_chk[kc])]
                if msk_v:
                    wpv.append((sMsk, msk_v))
                if kc == 0:
                    wpv.append((sNv, nrm_chk[hg]))

                def _pv(e, ptsl2=ptsl2, kc=kc, qoff=qoff, n=n, hg=hg):
                    stop = (kc == kc_last[qoff // 512])
                    return e.matmul(ps_o[0:65, qoff:qoff + n],
                                    V_sb[:, 1040 * kc + 65 * hg:1040 * kc + 65 * hg + 65],
                                    Pt[ptsl2][:, 0:n], start=(kc == 0), stop=stop)
                EM("tensor", wpv, [(sPV, 1)], _pv)
                pt_read[ptsl2] = sPV.n

            pend2 = []
            for (kc, pi, qoff, n) in plist:
                srcm = maskA if kc < 16 else maskB
                ptsl = gpar[0] % 3

                def _smm(e, pb, hh=hh, kc=kc, qoff=qoff, n=n, psl=psl):
                    return e.matmul(pb[:, 0:n],
                                    K_pair[psl][64 * hh:64 * (hh + 1),
                                                128 * kc:128 * (kc + 1)],
                                    Q_pair[psl][64 * hh:64 * (hh + 1), qoff:qoff + n],
                                    start=True, stop=True)
                par = pipe_mm([(sKQe, kqe_chk[pr]), (sNs, mu_s)], _smm, sS)
                wexp = [(sPV, pt_read[ptsl])]

                def _exp(e, pb, ptsl=ptsl, n=n, hg=hg):
                    return e.activation(out=Pt[ptsl][:, 0:n], in_=pb[:, 0:n], func=AF.Exp,
                                        scale=0.125, bias=negmu[:, hg:hg + 1])
                pipe_consume(par, wexp, _exp)
                exp_v = sExp.n
                msk_v = 0
                if pi == 0:
                    EM("vector", [(sExp, exp_v)], [(sMsk, 1)],
                       lambda e, ptsl=ptsl, srcm=srcm: e.tensor_tensor(
                           out=Pt[ptsl][:, 0:128], in0=Pt[ptsl][:, 0:128],
                           in1=srcm[:, :], op=ALU.mult))
                    msk_v = sMsk.n
                pend2.append((ptsl, kc, qoff, n, exp_v, msk_v))
                if len(pend2) > 2:
                    emit_pv(pend2.pop(0))
            for item in pend2:
                emit_pv(item)
            head_pv = sPV.n
            for m_ in range(4):
                EM("vector", [(sPV, head_pv)], [(sLv, 1)],
                   lambda e, m_=m_: e.reciprocal(out=linv[0:1, :],
                                                 in_=ps_o[64:65, 512 * m_:512 * (m_ + 1)]))
                lv_v = sLv.n

                def _lbmm(e, pb):
                    return e.matmul(pb[0:64, 0:512], ones64[0:1, :], linv[0:1, :],
                                    start=True, stop=True)
                par = pipe_mm([(sLv, lv_v)], _lbmm, sNt)
                lbsl = 0
                wlb = []
                if m_ >= 1:
                    wlb.append((sNv, sNv.n))

                def _lbev(e, pb, lbsl=lbsl):
                    return e.copy(out=lb_sb[lbsl][:, :], in_=pb[0:64, 0:512])
                pipe_consume(par, wlb, _lbev)
                ev_v = sExp.n
                wev = [(sExp, ev_v)] + ([(sC, rwt_c)] if pr == 0 else [])
                EM("vector", wev, [(sNv, 1)],
                   lambda e, m_=m_, hh=hh, pr=pr, lbsl=lbsl: e.tensor_tensor(
                       out=outf[64 * hh:64 * (hh + 1),
                                2048 * pr + 512 * m_:2048 * pr + 512 * (m_ + 1)],
                       in0=ps_o[0:64, 512 * m_:512 * (m_ + 1)],
                       in1=lb_sb[lbsl][:, :], op=ALU.mult))
            nrm_chk[hg + 1] = sNv.n
    attn_nrm = sNv.n
    attn_exp = sExp.n

    if _DEBUG:
        EM("sync", [(sNv, attn_nrm)], [(sDbg, 16)],
           lambda e: e.dma_start(out=dbg["outf_dbg"].ap(), in_=outf[:, :]))

    EM("sync", [(sNv, attn_nrm)], [(sW, 16)],
       lambda e: e.dma_start(out=bout_sb[:, :], in_=d["bout_b"].ap()))
    bout_w = sW.n
    for si, k_ in enumerate((2, 3)):
        EM("sync", [(sKQ, sKQ.n)], [(sWoS[si], 16)],
           lambda e, k_=k_, si=si: e.dma_start(
               out=slot_f32[k_][:, :],
               in_=_ap(d["wout"], 512 * si * 1024, [[1024, 128], [128 * 1024, 4], [1, 1024]])))
    wo_wsl = {2: sWoS[0].n, 3: sWoS[1].n}
    wout_cast = [0] * KC
    for c_ in range(KC):
        k_ = 2 if c_ < 4 else 3
        EM("scalar", [(sWoS[k_ - 2], wo_wsl[k_]), (sExp, attn_exp)], [(sC, 1)],
           lambda e, c_=c_, k_=k_: e.copy(
               out=wout_bf[:, 1024 * c_:1024 * (c_ + 1)],
               in_=slot_f32[k_][:, 1024 * (c_ % 4):1024 * (c_ % 4 + 1)]))
        wout_cast[c_] = sC.n

    acc_chk = [0] * NOWN

    def emit_yproj(j):
        sl = j % 2
        w = [(sC, wout_cast[KC - 1])]
        if j == 0:
            w.append((sNv, attn_nrm))
        if j >= 2:
            w.append((sAcc, acc_chk[j - 2]))

        def _yproj(e, sl=sl, j=j):
            r = None
            for pr2 in range(NPAIR):
                lhs = outf[:, 2048 * pr2 + 128 * j:2048 * pr2 + 128 * (j + 1)]
                for hf in range(2):
                    r = e.matmul(ps_o[:, 1024 * sl + 512 * hf:1024 * sl + 512 * (hf + 1)],
                                 lhs,
                                 wout_bf[:, 1024 * pr2 + 512 * hf:1024 * pr2 + 512 * (hf + 1)],
                                 start=(pr2 == 0), stop=(pr2 == NPAIR - 1))
            return r
        EM("tensor", w, [(sY, 1)], _yproj)
        y_v = sY.n
        EM("vector", [(sY, y_v), (sW, bout_w)], [(sAcc, 1)],
           lambda e, sl=sl, j=j: e.scalar_tensor_tensor(
               out=acc[:, 1024 * j:1024 * (j + 1)],
               in0=ps_o[:, 1024 * sl:1024 * (sl + 1)], scalar=0.0,
               in1=bout_sb[:, :], op0=ALU.add, op1=ALU.add))
        acc_chk[j] = sAcc.n

    rt_lg = [0] * NOWN
    rt_sg = [0] * NOWN
    rt_rv = [0] * NOWN
    rt_vc = [0]
    attn_drain = [(sExp, attn_exp), (sNv, attn_nrm), (sNs, sNs.n), (sKQe, sKQe.n)]

    def _route_j(j):
        def _route(e):
            g8 = rt_g8
            scj = rt_sc[j % 2]

            def step(inst):
                inst.then_inc(sRtC.h, 1)
                rt_vc[0] += 1
                e.wait_ge(sRtC.h, rt_vc[0])
                return inst
            step(e.tensor_tensor(out=rt_sfc[:, :], in0=scj[:, :], in1=cbb_sb[:, :], op=ALU.add))
            sfc3 = rt_sfc[:, :].rearrange("p (g x) -> p g x", x=8)
            step(e.reduce_max(out=g8[:, 0:8], in_=sfc3, axis=AX.X))
            step(e.tensor_tensor(out=rt_s64[:, :].rearrange("p (g x) -> p g x", x=8),
                                 in0=sfc3, in1=g8[:, 0:8].unsqueeze(2).broadcast_to((128, 8, 8)),
                                 op=ALU.is_ge))
            step(e.reduce_sum(out=g8[:, 8:16],
                              in_=rt_s64[:, :].rearrange("p (g x) -> p g x", x=8), axis=AX.X))
            step(e.scalar_tensor_tensor(out=rt_cur[:, :], in0=rt_s64[:, :], scalar=-1e30,
                                        in1=rt_sfc[:, :], op0=ALU.mult, op1=ALU.add))
            step(e.reduce_max(out=g8[:, 16:24],
                              in_=rt_cur[:, :].rearrange("p (g x) -> p g x", x=8), axis=AX.X))
            step(e.tensor_single_scalar(out=g8[:, 24:32], in_=g8[:, 8:16], scalar=2.0, op=ALU.is_ge))
            step(e.tensor_tensor(out=g8[:, 32:40], in0=g8[:, 0:8], in1=g8[:, 16:24], op=ALU.subtract))
            step(e.tensor_tensor(out=g8[:, 32:40], in0=g8[:, 32:40], in1=g8[:, 24:32], op=ALU.mult))
            step(e.tensor_tensor(out=g8[:, 32:40], in0=g8[:, 32:40], in1=g8[:, 16:24], op=ALU.add))
            step(e.tensor_tensor(out=g8[:, 40:48], in0=g8[:, 0:8], in1=g8[:, 32:40], op=ALU.add))
            for jj in range(8):
                e.tensor_scalar(out=rt_s64[:, 8 * jj:8 * (jj + 1)], in0=g8[:, 40:48],
                                scalar1=g8[:, 40 + jj:41 + jj], scalar2=None, op0=ALU.is_lt)
            step(e.tensor_single_scalar(out=rt_s64[:, 0:8], in_=rt_s64[:, 0:8],
                                        scalar=0.0, op=ALU.add))
            step(e.reduce_sum(out=g8[:, 48:56],
                              in_=_ap(rt_s64.ap().tensor, 0, [_pdim(rt_s64), [1, 8], [8, 8]]),
                              axis=AX.X))
            step(e.tensor_single_scalar(out=g8[:, 56:64], in_=g8[:, 48:56], scalar=3.0, op=ALU.is_le))
            gm3 = g8[:, 56:64].unsqueeze(2).broadcast_to((128, 8, 8))
            step(e.tensor_tensor(out=rt_cur[:, :].rearrange("p (g x) -> p g x", x=8),
                                 in0=sfc3, in1=gm3, op=ALU.mult))
            for it in range(8):
                step(e.reduce_max(out=rt_th[:, :], in_=rt_cur[:, :], axis=AX.X))
                if it < 7:
                    step(e.tensor_scalar(out=rt_s64[:, :], in0=rt_cur[:, :],
                                         scalar1=rt_th[:, 0:1], scalar2=None, op0=ALU.is_ge))
                    step(e.scalar_tensor_tensor(out=rt_cur[:, :], in0=rt_s64[:, :], scalar=-1e30,
                                                in1=rt_cur[:, :], op0=ALU.mult, op1=ALU.add))
            step(e.tensor_tensor(out=rt_s64[:, :].rearrange("p (g x) -> p g x", x=8),
                                 in0=sfc3, in1=gm3, op=ALU.mult))
            step(e.tensor_scalar(out=rt_s64[:, :], in0=rt_s64[:, :], scalar1=rt_th[:, 0:1],
                                 scalar2=None, op0=ALU.is_ge))
            step(e.tensor_tensor(out=rt_cur[:, :], in0=scj[:, :], in1=rt_s64[:, :], op=ALU.mult))
            step(e.reduce_sum(out=rt_ds[:, :], in_=rt_cur[:, :], axis=AX.X))
            step(e.tensor_single_scalar(out=rt_ds[:, :], in_=rt_ds[:, :], scalar=1e-20, op=ALU.add))
            step(e.reciprocal(out=rt_ds[:, :], in_=rt_ds[:, :]))
            return e.tensor_scalar(out=rt_fin[j][:, :], in0=rt_cur[:, :],
                                   scalar1=rt_ds[:, 0:1], scalar2=MOE_SCALE,
                                   op0=ALU.mult, op1=ALU.mult)
        return _route

    for j in range(NOWN):
        co = 64 * (j % 2)
        wlg = [(sC, rwt_c)]
        if j < 2:
            wlg += attn_drain
        else:
            wlg.append((sRtS, rt_sg[j - 2]))

        def _lg(e, j=j, co=co):
            r = None
            for c_ in range(KC):
                r = e.matmul(ps_s[0][:, co:co + 64],
                             xn_own[:, 2048 * c_ + 128 * j:2048 * c_ + 128 * (j + 1)],
                             rwt_bf[:, 64 * c_:64 * (c_ + 1)],
                             start=(c_ == 0), stop=(c_ == KC - 1))
            return r
        EM("tensor", wlg, [(sRtT, 1)], _lg)
        rt_lg[j] = sRtT.n
        wsg = [(sRtT, rt_lg[j])]
        if j >= 2:
            wsg.append((sRtV, rt_rv[j - 2]))
        EM("scalar", wsg, [(sRtS, 1)],
           lambda e, j=j, co=co: e.activation(out=rt_sc[j % 2][:, :],
                                              in_=ps_s[0][:, co:co + 64], func=AF.Sigmoid))
        rt_sg[j] = sRtS.n
        EM("vector", [(sRtS, rt_sg[j]), (sW, consts_w)], [(sRtV, 1)], _route_j(j))
        rt_rv[j] = sRtV.n
        if j >= 2:
            emit_yproj(j - 2)
    emit_yproj(NOWN - 2)
    emit_yproj(NOWN - 1)
    acc_init = sAcc.n
    yproj_done = sY.n

    if _DEBUG:
        EM("sync", [(sAcc, acc_init)], [(sDbg, 16)],
           lambda e: e.dma_start(out=dbg["acc3_dbg"].ap(), in_=acc[:, :]))

    rt_ct_chk = [0] * NOWN
    rt_ev_chk = [0] * NOWN
    rt_cb_chk = [0] * NOWN
    cmb_tile = [None] * NOWN
    for j in range(NOWN):
        wt_ = [(sRtV, rt_rv[j])]
        if j >= 1:
            wt_.append((sRtS, rt_cb_chk[j - 1]))
        EM("tensor", wt_, [(sRtT, 1)],
           lambda e, j=j: e.transpose(out=ps_tb[1][0:64, 0:128], in_=rt_fin[j][:, :],
                                      identity=identb[:, :]))
        rt_ct_chk[j] = sRtT.n
        csl = j % 2
        wce = [(sRtT, rt_ct_chk[j])]
        if j >= 2:
            wce.append((sCmb[csl], rt_ev_chk[j - 2]))
        EM("scalar", wce, [(sRtS, 1)],
           lambda e, csl=csl: e.copy(out=rt_cbtile[csl][:, :], in_=ps_tb[1][0:64, 0:128]))
        rt_cb_chk[j] = sRtS.n
        EM("scalar", [(sRtS, rt_cb_chk[j])], [(sCmb[csl], 16)],
           lambda e, csl=csl, j=j: e.dma_start(
               out=_ap(comb_dram, 128 * j, [[OWN, 64], [1, 128]]),
               in_=rt_cbtile[csl][:, :]))
        rt_ev_chk[j] = sCmb[csl].n
        cmb_tile[j] = [(sCmb[0], sCmb[0].n), (sCmb[1], sCmb[1].n)]
    comb_done = [(sCmb[0], sCmb[0].n), (sCmb[1], sCmb[1].n)]
    rt_final_evac = sRtS.n
    rt_lg_done = sRtT.n

    if _DEBUG:
        EM("sync", comb_done, [(sDbg, 16)],
           lambda e: e.dma_start(out=dbg["comb_dbg"].ap(), in_=comb_dram.ap()))

    pex = [("r", i) for i in range(E)] + [("s", i) for i in range(4)]
    up_cast = [0] * NEX
    dn_cast = [0] * NEX
    dn_ws = [0]
    dn_ws_cast = []

    def emit_loads_pair(ea, eb):
        descs = []
        for ei in (ea, eb):
            if ei is None or ei >= NEX:
                continue
            kind, idx = pex[ei]
            k = (ei + 2) % 4
            if kind == "r":
                srcu4 = [_ap(d["upw"], idx * C * IM + 2 * q_ * 128 * IM,
                             [[IM, 128], [128 * IM, 2], [1, IM]]) for q_ in range(4)]
                dsc = [_ap(d["dnw"], idx * IM * C + p_ * 128 * C, [[C, 128], [1, C]])
                       for p_ in range(4)]
            else:
                srcu4 = [_ap(d["shup"], idx * IM + 2 * q_ * 128 * ISH,
                             [[ISH, 128], [128 * ISH, 2], [1, IM]]) for q_ in range(4)]
                dsc = [_ap(d["shdn"], (idx * IM + p_ * 128) * C, [[C, 128], [1, C]])
                       for p_ in range(4)]
            w = []
            if ei >= 4:
                w.append((sDn, 16 * ((ei - 4) // 2 + 1)))
            else:
                w.append((sY, yproj_done) if k < 2
                         else (sC, wout_cast[3] if k == 2 else wout_cast[KC - 1]))
            for q_ in range(4):
                EM("sync", w if q_ == 0 else [], [(sWsl[k], 16)],
                   lambda e, s=srcu4[q_], k=k, q_=q_: e.dma_start(
                       out=slot_f32[k][:, 1024 * q_:1024 * (q_ + 1)], in_=s))
            descs.append((ei, k, dsc, sWsl[k].n))
        for ei, k, dsc, upw_v in descs:
            EM("scalar", [(sWsl[k], upw_v)], [(sCg, 1)],
               lambda e, k=k: e.copy(out=up_bf[k][:, :], in_=slot_f32[k][:, :]))
            up_cast[ei] = sCg.n
        for ei, k, dsc, upw_v in descs:
            for p_ in range(4):
                wsl = dn_ws[0] % 4
                wd = []
                if dn_ws[0] >= 4:
                    wd.append((sCg, dn_ws_cast[dn_ws[0] - 4]))
                elif dn_ws[0] in (0, 1):
                    wd.append((sC, wout_cast[KC - 1]))
                elif dn_ws[0] in (2, 3):
                    wd.append((sRtT, rt_lg_done))
                odd_q = False
                dn_eng = "scalar" if odd_q else "sync"
                dn_sem = sWs2kB if odd_q else sWs2k
                EM(dn_eng, wd, [(dn_sem, 16)],
                   lambda e, s=dsc[p_], wsl=wsl: e.dma_start(out=ws4[wsl][:, :], in_=s))
                d_v = dn_sem.n
                wc = [(dn_sem, d_v)]
                if p_ == 0:
                    wc.append((sCg, up_cast[ei]))
                EM("scalar", wc, [(sCg, 1)],
                   lambda e, k=k, p_=p_, wsl=wsl: e.copy(
                       out=dn_bf[k][:, 1024 * p_:1024 * (p_ + 1)], in_=ws4[wsl][:, :]))
                dn_ws_cast.append(sCg.n)
                dn_ws[0] += 1
            dn_cast[ei] = sCg.n

    emit_loads_pair(0, 1)
    emit_loads_pair(2, 3)

    cb_sched = []
    for g in range(NEX // 2):
        for tq in range(4):
            for eo in range(2):
                ei = 2 * g + eo
                if pex[ei][0] == "r":
                    cb_sched.append((ei, tq))
    cb_consume_after = [0] * len(cb_sched)
    cb_w = {}
    cb_issued = [0]

    def issue_cb(upto):
        while cb_issued[0] < min(upto, len(cb_sched)):
            kk = cb_issued[0]
            ei, tq = cb_sched[kk]
            idx = pex[ei][1]
            slot = kk % 4
            w = list(cmb_tile[4 * tq + 3])
            if kk >= 4:
                w.append((sH, cb_consume_after[kk - 4]))
            EM("scalar", w, [(sCb4[slot], 16)],
               lambda e, idx=idx, tq=tq, slot=slot: e.dma_start(
                   out=cb_sb[slot][:, :],
                   in_=_ap(comb_dram, idx * OWN + 512 * tq, [[0, 128], [1, 512]])))
            cb_w[(ei, tq)] = (sCb4[slot], sCb4[slot].n)
            cb_issued[0] += 1

    cb_pos = [0]
    hs_cnt = [0]
    st_up = [0]
    st_rel = [0]
    st_dn = [0]

    dn_after_q = {}
    h_ready_q = {}

    def emit_up_quarter(qi):
        g, tq = qi // 4, qi % 4
        e0, e1 = 2 * g, 2 * g + 1
        nonlocal_state = None
        for eo, ei in enumerate((e0, e1)):
            kind, idx = pex[ei]
            k = (ei + 2) % 4
            hbuf = h_sb[2 * (tq % 2) + eo]
            if kind == "r":
                issue_cb(cb_pos[0] + 4)
                cslot = cb_pos[0] % 4
            for ic in range(4):
                pb = ps_a if st_up[0] % 2 == 0 else ps_b
                w = [(sCg, up_cast[ei])]
                if st_up[0] >= 2:
                    w.append((sRel, st_rel[0] - 1))

                def _up(e, pb=pb, k=k, ic=ic, tq=tq):
                    r = None
                    for c_ in range(KC):
                        r = e.matmul(
                            pb[:, 0:512],
                            up_bf[k][:, 512 * c_ + 128 * ic:512 * c_ + 128 * (ic + 1)],
                            xn_own[:, 2048 * c_ + 512 * tq:2048 * c_ + 512 * (tq + 1)],
                            start=(c_ == 0), stop=(c_ == KC - 1))
                    return r
                EM("tensor", w, [(sUp, 1)], _up)
                st_up[0] += 1
                up_v = sUp.n
                wr = [(sUp, up_v)]
                if qi >= 2:
                    wr.append((sDn, dn_after_q[qi - 2]))
                EM("scalar", wr, [(sRel, 1)],
                   lambda e, pb=pb, hbuf=hbuf, ic=ic: e.activation(
                       out=hbuf[:, 512 * ic:512 * (ic + 1)], in_=pb[:, 0:512],
                       func=AF.Relu))
                st_rel[0] += 1
                rel_v = sRel.n
                if kind == "r":
                    tsl = hs_cnt[0] % 2
                    wv_ = [(sRel, rel_v), cb_w[(ei, tq)]]
                    if hs_cnt[0] >= 2:
                        wv_.append((sH, sH.n - 1))
                    EM("vector", wv_, [(sVs, 1)],
                       lambda e, hbuf=hbuf, ic=ic, cslot=cslot, tsl=tsl:
                       e.tensor_tensor(out=hs_t1[tsl][:, :],
                                       in0=hbuf[:, 512 * ic:512 * (ic + 1)],
                                       in1=cb_sb[cslot][:, :], op=ALU.mult))
                    EM("gpsimd", [(sVs, sVs.n)], [(sH, 1)],
                       lambda e, hbuf=hbuf, ic=ic, tsl=tsl:
                       e.tensor_tensor(out=hbuf[:, 512 * ic:512 * (ic + 1)],
                                       in0=hs_t1[tsl][:, :],
                                       in1=hbuf[:, 512 * ic:512 * (ic + 1)],
                                       op=ALU.mult))
                    hs_cnt[0] += 1
                else:
                    EM("vector", [(sRel, rel_v)], [(sH, 1)],
                       lambda e, hbuf=hbuf, ic=ic:
                       e.tensor_tensor(out=hbuf[:, 512 * ic:512 * (ic + 1)],
                                       in0=hbuf[:, 512 * ic:512 * (ic + 1)],
                                       in1=hbuf[:, 512 * ic:512 * (ic + 1)],
                                       op=ALU.mult))
            if kind == "r":
                cb_consume_after[cb_pos[0]] = sH.n
                cb_pos[0] += 1
        h_ready_q[qi] = sH.n

    def emit_down_quarter(qi):
        g, tq = qi // 4, qi % 4
        e0, e1 = 2 * g, 2 * g + 1
        for tt in range(4):
            pd_off = 1024 * (tt % 2)
            w = [(sH, h_ready_q[qi]), (sCg, max(dn_cast[e0], dn_cast[e1]))]
            if st_dn[0] >= 2:
                w.append((sDnE, sDnE.n - 1))
            if qi == 0:
                w.append((sAcc, acc_init))

            def _dn(e, e0=e0, e1=e1, tq=tq, tt=tt, pd_off=pd_off):
                r = None
                for eo2, ei2 in enumerate((e0, e1)):
                    k2 = (ei2 + 2) % 4
                    hb = h_sb[2 * (tq % 2) + eo2]
                    last_e = ei2 == e1
                    for ic in range(4):
                        lhs = hb[:, 512 * ic + 128 * tt:512 * ic + 128 * (tt + 1)]
                        for hf in range(2):
                            r = e.matmul(
                                ps_o[:, pd_off + 512 * hf:pd_off + 512 * (hf + 1)],
                                lhs,
                                dn_bf[k2][:, 1024 * ic + 512 * hf:1024 * ic + 512 * (hf + 1)],
                                start=(eo2 == 0 and ic == 0),
                                stop=(last_e and ic == 3))
                return r
            EM("tensor", w, [(sDn, 1)], _dn)
            st_dn[0] += 1
            dn_v = sDn.n
            tile = 4 * tq + tt
            EM("vector", [(sDn, dn_v)], [(sDnE, 1)],
               lambda e, pd_off=pd_off, tile=tile: e.scalar_tensor_tensor(
                   out=acc[:, 1024 * tile:1024 * (tile + 1)],
                   in0=ps_o[:, pd_off:pd_off + 1024], scalar=0.0,
                   in1=acc[:, 1024 * tile:1024 * (tile + 1)],
                   op0=ALU.add, op1=ALU.add))
        dn_after_q[qi] = sDn.n
        if tq == 3:
            emit_loads_pair(2 * g + 4, 2 * g + 5)

    NQ = 4 * (NEX // 2)
    emit_up_quarter(0)
    for qi in range(1, NQ):
        emit_up_quarter(qi)
        emit_down_quarter(qi - 1)
    emit_down_quarter(NQ - 1)
    moe_all = sDnE.n

    adds = [0] * NOWN
    last_st = 0
    for j in range(NOWN):
        sl = j % 2
        w = [(sDnE, (NEX // 2 - 1) * 16 + j + 1), (sCg, sCg.n)]
        if j >= 2:
            w.append((sAcc, adds[j - 2]))
        EM("sync", w, [(sXr2[sl], 16)],
           lambda e, sl=sl, j=j: e.dma_start(out=x_stage[sl][:, :],
                                             in_=xp_t.ap()[128 * j:128 * (j + 1), :]))
        xr_v = sXr2[sl].n
        wv_ = [(sXr2[sl], xr_v)]
        if j >= 1:
            wv_.append((sSt, last_st))
        EM("vector", wv_, [(sAcc, 1)],
           lambda e, sl=sl, j=j: e.tensor_tensor(out=ob_sb[:, :],
                                                 in0=acc[:, 1024 * j:1024 * (j + 1)],
                                                 in1=x_stage[sl][:, :], op=ALU.add))
        adds[j] = sAcc.n
        EM("sync", [(sAcc, adds[j])], [(sSt, 16)],
           lambda e, j=j: e.dma_start(out=d["out"].ap()[128 * j:128 * (j + 1), :],
                                      in_=ob_sb[:, :]))
        last_st = sSt.n

    EM("sync", [(sSt, sSt.n), (sDbg, sDbg.n)], [], lambda e: e.nop())

    with nc.Block() as block:
        @block.sync
        def _(e):
            for fn in p.em["sync"]:
                fn(e)

        @block.tensor
        def _(e):
            for fn in p.em["tensor"]:
                fn(e)

        @block.vector
        def _(e):
            for fn in p.em["vector"]:
                fn(e)

        @block.scalar
        def _(e):
            for fn in p.em["scalar"]:
                fn(e)

        @block.gpsimd
        def _(e):
            for fn in p.em["gpsimd"]:
                fn(e)

    return nc


def _perm_rows(s):
    own = [2 * i + s for i in range(16)]
    oth = [2 * i + (1 - s) for i in range(16)]
    rows = []
    for t_ in own + oth:
        rows.extend(range(128 * t_, 128 * (t_ + 1)))
    return np.array(rows, dtype=np.int64)


def kernel(L0_attn_x, L0_attn_w_qkv, L0_attn_b_qkv, L0_attn_w_out, L0_attn_b_out,
           L0_ffn_router_w, L0_ffn_correction_bias, L0_ffn_up_w, L0_ffn_down_w,
           L0_ffn_shared_up_w, L0_ffn_shared_down_w):
    x = np.ascontiguousarray(np.asarray(L0_attn_x, dtype=np.float32))
    wqkv = np.ascontiguousarray(np.asarray(L0_attn_w_qkv, dtype=np.float32))
    bqkv = np.asarray(L0_attn_b_qkv, dtype=np.float32)
    wout = np.ascontiguousarray(np.asarray(L0_attn_w_out, dtype=np.float32))
    bout = np.asarray(L0_attn_b_out, dtype=np.float32)
    rw = np.asarray(L0_ffn_router_w, dtype=np.float32)
    cbias = np.asarray(L0_ffn_correction_bias, dtype=np.float32)
    upw = np.ascontiguousarray(np.asarray(L0_ffn_up_w, dtype=np.float32))
    dnw = np.ascontiguousarray(np.asarray(L0_ffn_down_w, dtype=np.float32))
    shup = np.ascontiguousarray(np.asarray(L0_ffn_shared_up_w, dtype=np.float32))
    shdn = np.ascontiguousarray(np.asarray(L0_ffn_shared_down_w, dtype=np.float32))

    bqkv_pp = np.ascontiguousarray(bqkv.reshape(24, 128).T)
    bout_b = np.ascontiguousarray(np.broadcast_to(bout[None, :], (128, C)))
    bvb = np.ascontiguousarray(np.broadcast_to(bqkv[None, 2048:3072], (128, C)))
    rwt = np.ascontiguousarray(rw.T)
    cbb = np.ascontiguousarray(np.broadcast_to(cbias[None, :], (128, E)))
    tri = np.ascontiguousarray(
        np.triu(np.ones((128, 128), np.float32)).astype(ml_dtypes.bfloat16))
    zeros128 = np.zeros((128, 128), ml_dtypes.bfloat16)
    ones128 = np.ones((128, 128), ml_dtypes.bfloat16)
    identb = np.eye(128, dtype=ml_dtypes.bfloat16)
    identf = np.eye(128, dtype=np.float32)
    ones64 = np.ones((1, 64), np.float32)
    negeighth = np.full((1, 128), -0.125, np.float32)

    nc = _build()

    in_maps = []
    for c in range(8):
        b, s = c // 2, c % 2
        pr = _perm_rows(s)
        in_maps.append({
            "xp": np.ascontiguousarray(x[b][pr]),
            "wqkv": wqkv, "bqkv_pp": bqkv_pp, "wout": wout, "bout_b": bout_b,
            "bvb": bvb, "rwt": rwt, "cbb": cbb, "upw": upw, "dnw": dnw,
            "shup": shup, "shdn": shdn,
            "maskA": tri, "maskB": (zeros128 if s == 0 else ones128),
            "identb": identb, "identf": identf, "ones64": ones64,
            "negeighth": negeighth,
        })

    res = run_bass_kernel_spmd(nc, in_maps, core_ids=list(range(8)))

    out = np.empty((B, T, C), np.float32)
    for c in range(8):
        b, s = c // 2, c % 2
        o = res.results[c]["out"]
        for i in range(16):
            out[b, 128 * (2 * i + s):128 * (2 * i + s + 1)] = o[128 * i:128 * (i + 1)]
    return out
